# version 10
# baseline (speedup 1.0000x reference)
import numpy as np
import ml_dtypes
import concourse.bass as bass
import concourse.mybir as mybir
from concourse.bass_utils import run_bass_kernel_spmd

BF16NP = ml_dtypes.bfloat16
F32 = mybir.dt.float32
BF16 = mybir.dt.bfloat16
U8 = mybir.dt.uint8
AF = mybir.ActivationFunctionType
ALU = mybir.AluOpType
AX = mybir.AxisListType

L = 2048
D = 1024
H = 16
DH = 64
NCHUNK = D // 128
NIC = L // 128
NIG = L // 512
NEG = -1.0e9
EPS = 1e-5

_NC_CACHE = {}


def _ap_box(ap):
    lo = ap.offset
    hi = lo
    for (step, count) in ap.ap:
        hi += abs(step) * (count - 1)
    return (ap.tensor.name, lo, hi + 1)


class P:

    def __init__(self, nc, serial=False):
        self.nc = nc
        self.serial = serial
        self.steps = []

    def _add(self, eng, inc, fn, reads, writes):
        self.steps.append((eng, inc, fn, reads, writes))

    def barrier(self):
        self.steps.append(("BARRIER", 0, None, None, None))

    def dma(self, out, in_, eng="sync", **kw):
        self._add(eng, 16,
                  lambda e, out=out, in_=in_, kw=kw: e.dma_start(out=out, in_=in_, **kw),
                  [in_], [out])

    def mm(self, out, lhsT, rhs, start=True, stop=True):
        reads = [lhsT, rhs] + ([out] if not start else [])
        self._add("tensor", 1,
                  lambda e, out=out, l=lhsT, r=rhs, s=start, st=stop: e.matmul(out, l, r, start=s, stop=st),
                  reads, [out])

    def act(self, out, in_, func=AF.Copy, bias=0.0, scale=1.0):
        if func == AF.Copy and not isinstance(bias, float):
            func = AF.Identity
        reads = [in_] + [a for a in (bias, scale) if not isinstance(a, float)]
        self._add("scalar", 1,
                  lambda e, out=out, i=in_, f=func, b=bias, s=scale: e.activation(out=out, in_=i, func=f, bias=b, scale=s),
                  reads, [out])

    def tt(self, out, in0, in1, op, eng="vector"):
        self._add(eng, 1,
                  lambda e, out=out, a=in0, b=in1, op=op: e.tensor_tensor(out, a, b, op),
                  [in0, in1], [out])

    def stt(self, out, in0, scalar, in1, op0, op1, eng="vector"):
        self._add(eng, 1,
                  lambda e, out=out, a=in0, s=scalar, b=in1, o0=op0, o1=op1: e.scalar_tensor_tensor(out, a, s, b, o0, o1),
                  [in0, scalar, in1], [out])

    def tsc(self, out, in0, scalar1, scalar2, op0, op1=ALU.bypass, eng="vector"):
        reads = [in0] + [a for a in (scalar1, scalar2) if a is not None and not isinstance(a, float)]
        self._add(eng, 1,
                  lambda e, out=out, a=in0, s1=scalar1, s2=scalar2, o0=op0, o1=op1: e.tensor_scalar(out, a, s1, s2, o0, o1),
                  reads, [out])

    def vcopy(self, out, in_, eng="vector"):
        self._add(eng, 1, lambda e, out=out, i=in_: e.tensor_copy(out=out, in_=i),
                  [in_], [out])

    def recip(self, out, in_):
        self._add("vector", 1, lambda e, out=out, i=in_: e.reciprocal(out=out, in_=i),
                  [in_], [out])

    def rmax(self, out, in_):
        self._add("vector", 1, lambda e, out=out, i=in_: e.reduce_max(out, i, axis=AX.X),
                  [in_], [out])

    def memset(self, ap, val, eng="gpsimd"):
        self._add(eng, 1, lambda e, ap=ap, v=val: e.memset(ap, v), [], [ap])

    def build(self):
        nc = self.nc
        steps = self.steps
        n = len(steps)
        KSLOT = 8
        sem_of = [None] * n
        count_after = [0] * n
        sem_tot = {}
        dma_idx = {}
        for i, st in enumerate(steps):
            if st[0] == "BARRIER":
                continue
            eng, inc = st[0], st[1]
            if inc == 16:
                k = dma_idx.get(eng, 0)
                dma_idx[eng] = k + 1
                sid = ("d", eng, k % KSLOT)
            else:
                sid = ("c", eng)
            sem_of[i] = sid
            sem_tot[sid] = sem_tot.get(sid, 0) + inc
            count_after[i] = sem_tot[sid]

        deps = [None] * n
        writers = {}
        readers = {}
        last_dma = {}
        snap_at_barrier = []
        barrier_needs = {}
        pending_snapshot = None
        for i, st in enumerate(steps):
            if st[0] == "BARRIER":
                pending_snapshot = dict(sem_tot_run := {})
                snap = {}
                for j in range(i):
                    if sem_of[j] is not None:
                        s = sem_of[j]
                        if count_after[j] > snap.get(s, 0):
                            snap[s] = count_after[j]
                pending_snapshot = snap
                writers.clear()
                readers.clear()
                last_dma.clear()
                continue
            eng, inc, fn, rd, wr = st
            d = set()
            if inc == 16:
                sid = sem_of[i]
                if sid in last_dma:
                    d.add(last_dma[sid])
                last_dma[sid] = i
            for ap in rd:
                key, lo, hi = _ap_box(ap)
                for (j, l2, h2) in writers.get(key, ()):
                    if l2 < hi and lo < h2:
                        d.add(j)
            for ap in wr:
                key, lo, hi = _ap_box(ap)
                for (j, l2, h2) in writers.get(key, ()):
                    if l2 < hi and lo < h2:
                        d.add(j)
                for (j, l2, h2) in readers.get(key, ()):
                    if l2 < hi and lo < h2:
                        d.add(j)
            for ap in rd:
                key, lo, hi = _ap_box(ap)
                readers.setdefault(key, []).append((i, lo, hi))
            for ap in wr:
                key, lo, hi = _ap_box(ap)
                writers.setdefault(key, []).append((i, lo, hi))
            deps[i] = d
            if pending_snapshot is not None:
                barrier_needs[i] = pending_snapshot

        barrier_pos = [i for i, st in enumerate(steps) if st[0] == "BARRIER"]
        barrier_snaps = {}
        for i in barrier_pos:
            snap = {}
            for j in range(i):
                if sem_of[j] is not None:
                    s = sem_of[j]
                    if count_after[j] > snap.get(s, 0):
                        snap[s] = count_after[j]
            barrier_snaps[i] = snap

        sem_ids = sorted(sem_tot.keys())
        import contextlib as _ctx
        with _ctx.ExitStack() as es:
            sems = {s: es.enter_context(nc.semaphore("sem_" + "_".join(str(x) for x in s)))
                    for s in sem_ids}
            with nc.Block() as blk:
                def mk(name):
                    def sec(eng):
                        wm = {s: 0 for s in sem_ids}
                        own = ("c", name)
                        bi = 0
                        pend = {}
                        emitted = False
                        for i, st in enumerate(steps):
                            if st[0] == "BARRIER":
                                for s, v in barrier_snaps[i].items():
                                    if v > pend.get(s, 0):
                                        pend[s] = v
                                continue
                            if st[0] != name:
                                continue
                            need = dict(pend)
                            pend = {}
                            for j in (deps[i] or ()):
                                s = sem_of[j]
                                if s == own and name == "tensor":
                                    continue
                                v = count_after[j]
                                if v > need.get(s, 0):
                                    need[s] = v
                            if self.serial:
                                pass
                            for s in sem_ids:
                                v = need.get(s, 0)
                                if v > wm[s]:
                                    eng.wait_ge(sems[s], v)
                                    wm[s] = v
                            st[2](eng).then_inc(sems[sem_of[i]], st[1])
                            emitted = True
                        if name == "sync":
                            for s in sem_ids:
                                if sem_tot[s] > wm[s]:
                                    eng.wait_ge(sems[s], sem_tot[s])
                        _ = emitted
                    return sec
                for name in ("sync", "tensor", "scalar", "vector", "gpsimd"):
                    getattr(blk, name)(mk(name))


class NCP:

    def __init__(self, nc, pre):
        self._nc = nc
        self._pre = pre

    def sbuf_tensor(self, name, shape, dtype, **kw):
        return self._nc.sbuf_tensor(self._pre + name, shape, dtype, **kw)

    def psum_tensor(self, name, shape, dtype=F32, **kw):
        return self._nc.psum_tensor(self._pre + name, shape, dtype, **kw)

    def __getattr__(self, a):
        return getattr(self._nc, a)


def load_weight_bf16(p, nc, w_hbm, dst_bf, scr_f32, ncols, nchunks):
    half = scr_f32.shape[1] // 2
    i = 0
    for c in range(nchunks):
        for c0 in range(0, ncols, 1024):
            w = min(1024, ncols - c0)
            o = (i % 2) * half if w <= half else 0
            p.dma(scr_f32[:, o:o + w], w_hbm[c * 128:(c + 1) * 128, c0:c0 + w])
            p.vcopy(dst_bf[:, c * ncols + c0: c * ncols + c0 + w],
                    scr_f32[:, o:o + w], eng="gpsimd")
            i += 1


def load_xbf(p, nc, xT_hbm, xbf, scr_f32):
    for c in range(NCHUNK):
        p.dma(scr_f32[:, 0:L], xT_hbm[c * 128:(c + 1) * 128, :])
        p.vcopy(xbf[:, c * L:(c + 1) * L], scr_f32[:, 0:L], eng="gpsimd")


def emit_postln(p, nc, yscr_hbm, ybf, ones_bf, xout_hbm, tiles, fc):
    (ps_r1, ps_c, r_mean, r_var, r_rstd, r_mr, rowb_bf,
     Brstd, Bmr, chunk_scr, chunk_scr2) = tiles
    for ig in range(NIG):
        for c in range(NCHUNK):
            p.mm(ps_r1[:], ones_bf[:, 0:1], ybf[:, c * L + ig * 512: c * L + ig * 512 + 512],
                 start=(c == 0), stop=(c == NCHUNK - 1))
        p.act(r_mean[:, ig * 512:(ig + 1) * 512], ps_r1[:])
    for ig in range(NIG):
        for c in range(NCHUNK):
            sl = ybf[:, c * L + ig * 512: c * L + ig * 512 + 512]
            o = (c % 2) * 512
            p.act(chunk_scr[:, o:o + 512], sl, func=AF.Square)
            p.mm(ps_r1[:], ones_bf[:, 0:1], chunk_scr[:, o:o + 512],
                 start=(c == 0), stop=(c == NCHUNK - 1))
        p.act(r_var[:, ig * 512:(ig + 1) * 512], ps_r1[:])
    p.tsc(r_mean[:], r_mean[:], 1.0 / D, None, ALU.mult)
    p.tsc(r_var[:], r_var[:], 1.0 / D, None, ALU.mult)
    p.tt(r_mr[:], r_mean[:], r_mean[:], ALU.mult)
    p.tt(r_var[:], r_var[:], r_mr[:], ALU.subtract)
    p.act(r_var[:], r_var[:], func=AF.Sqrt, bias=fc[0:1, 3:4])
    p.recip(r_rstd[:], r_var[:])
    p.tt(r_mr[:], r_mean[:], r_rstd[:], ALU.mult)
    for (row, B) in ((r_rstd, Brstd), (r_mr, Bmr)):
        p.act(rowb_bf[:], row[:])
        for ig in range(NIG):
            p.mm(ps_c[:], ones_bf[0:1, 0:128], rowb_bf[:, ig * 512:(ig + 1) * 512])
            p.act(B[:, ig * 512:(ig + 1) * 512], ps_c[:])
    for i in range(2 * NCHUNK):
        c, h = divmod(i, 2)
        o = (i % 2) * 1024
        cs = slice(h * 1024, h * 1024 + 1024)
        p.dma(chunk_scr2[:, o:o + 1024], yscr_hbm[c * 128:(c + 1) * 128, cs])
        p.tt(chunk_scr2[:, o:o + 1024], chunk_scr2[:, o:o + 1024], Brstd[:, cs], ALU.mult)
        p.tt(chunk_scr2[:, o:o + 1024], chunk_scr2[:, o:o + 1024], Bmr[:, cs], ALU.subtract,
             eng="gpsimd")
        p.dma(xout_hbm[c * 128:(c + 1) * 128, cs], chunk_scr2[:, o:o + 1024], eng="gpsimd")


def alloc_ln_tiles(es, nc, chunk_scr2=None):
    ps_r1 = es.enter_context(nc.psum_tensor("ps_ln1", [1, 512], F32))
    ps_c = es.enter_context(nc.psum_tensor("ps_lnc", [128, 512], F32))
    r_mean = es.enter_context(nc.sbuf_tensor("ln_mean", [1, L], F32))
    r_var = es.enter_context(nc.sbuf_tensor("ln_var", [1, L], F32))
    r_rstd = es.enter_context(nc.sbuf_tensor("ln_rstd", [1, L], F32))
    r_mr = es.enter_context(nc.sbuf_tensor("ln_mr", [1, L], F32))
    rowb_bf = es.enter_context(nc.sbuf_tensor("ln_rowbf", [1, L], BF16))
    Brstd = es.enter_context(nc.sbuf_tensor("ln_Brstd", [128, L], F32))
    Bmr = es.enter_context(nc.sbuf_tensor("ln_Bmr", [128, L], F32))
    chunk_scr = es.enter_context(nc.sbuf_tensor("ln_cscr", [128, 1024], BF16))
    if chunk_scr2 is None:
        chunk_scr2 = es.enter_context(nc.sbuf_tensor("ln_cscr2", [128, L], F32))
    return (ps_r1, ps_c, r_mean, r_var, r_rstd, r_mr, rowb_bf,
            Brstd, Bmr, chunk_scr, chunk_scr2)


def emit_attention(p, nc, env, layer):
    (Vt, obf, qxA, qxB, kxA, kxB, sqh, nq_row, km, rcp, rcpb,
     Fsb, tmul, PT, ps_a, ps_o, ps_f, ps_r1) = (
        env["Vt"], env["obf"], env["qxA"], env["qxB"], env["kxA"], env["kxB"],
        env["sqh"], env["nq_row"], env["km"], env["rcp"],
        env["rcpb"], env["Fsb"], env["tmul"], env["PT"], env["ps_a"],
        env["ps_o"], env["ps_f"], env["ps_r1"])
    ones_bf = env["ones_bf"]
    psAB = [ps_a, env["ps_a2"]]
    PTs = [PT, env["PT2"]]
    psOs = [ps_o, env["ps_o2"]]
    tri = env.get("tri")
    causal = layer == 1
    gctr = [0]

    for hp in range(8):
        env["build_qk"](p, hp)
        for (qx, kx) in ((qxA, kxA), (qxB, kxB)):
            for ig in range(NIG):
                p.act(sqh[:, 0:512], kx[0:64, ig * 512:(ig + 1) * 512], func=AF.Square)
                p.mm(ps_r1[:], ones_bf[0:64, 0:1], sqh[:, 0:512])
                p.rmax(km[:, ig:ig + 1], ps_r1[:])
            p.rmax(km[:, 4:5], km[:, 0:4])
            p.act(km[:, 4:5], km[:, 4:5], func=AF.Sqrt)
            for ig in range(NIG):
                p.act(sqh[:, 0:512], qx[0:64, ig * 512:(ig + 1) * 512], func=AF.Square)
                p.mm(ps_r1[:], ones_bf[0:64, 0:1], sqh[:, 0:512])
                p.act(nq_row[:, 0:512], ps_r1[:], func=AF.Sqrt)
                p.tsc(qx[64:65, ig * 512:(ig + 1) * 512], nq_row[:, 0:512],
                      km[0:1, 4:5], -1.0, ALU.mult, ALU.mult)
            p.memset(kx[64:65, :], 1.0)
        for hh in range(2):
            h = 2 * hp + hh
            qx, kx = (qxA, kxA) if hh == 0 else (qxB, kxB)
            for ig in range(NIG):
                njc = min(NIC, 4 * (ig + 1)) if causal else NIC
                po = psOs[gctr[0] % 2]
                gctr[0] += 1
                for jc in range(njc + 1):
                    if jc < njc:
                        pa = psAB[jc % 2]
                        pt = PTs[jc % 2]
                        p.mm(pa[:], kx[:, jc * 128:(jc + 1) * 128],
                             qx[:, ig * 512:(ig + 1) * 512])
                        off = jc * 128 - ig * 512
                        if causal and off >= 0:
                            p.tt(pa[:, off:off + 128], pa[:, off:off + 128], tri[:], ALU.add)
                        p.act(pt[:], pa[:], func=AF.Exp, scale=env["fc"][:, 0:1])
                        if causal and off > 0:
                            p.memset(pt[:, 0:off], 0.0)
                    if jc >= 1:
                        jp = jc - 1
                        p.mm(po[:], Vt[:, jp * 1040 + 65 * h: jp * 1040 + 65 * h + 65],
                             PTs[jp % 2][:], start=(jp == 0), stop=(jp == njc - 1))
                p.recip(rcp[:], po[64:65, :])
                p.act(rcpb[:], rcp[:])
                p.mm(ps_f[:], ones_bf[0:1, 0:64], rcpb[:])
                p.act(Fsb[:], ps_f[:])
                p.tt(tmul[:], po[0:64, :], Fsb[:], ALU.mult)
                bv = env["bv"](hp, hh)
                dst = obf[64 * hh:64 * hh + 64,
                          hp * L + ig * 512: hp * L + ig * 512 + 512]
                if bv is None:
                    p.vcopy(dst, tmul[:], eng="gpsimd")
                else:
                    p.act(dst, tmul[:], bias=bv)


def emit_proj_residual(p, nc, env, w_bf, bT, xT_hbm, yscr_hbm, ybf):
    obf = env["obf"]
    psAB = [env["ps_a"], env["ps_a2"]]
    xchunk = env["xchunk"]
    ychunk = env["ychunk"]
    nxb = xchunk.shape[1] // L
    for oc in range(NCHUNK):
        xo = (oc % nxb) * L
        p.dma(xchunk[:, xo:xo + L], xT_hbm[oc * 128:(oc + 1) * 128, :])
        for ig in range(NIG):
            pa = psAB[ig % 2]
            for fc in range(NCHUNK):
                p.mm(pa[:], w_bf[:, fc * 1024 + oc * 128: fc * 1024 + oc * 128 + 128],
                     obf[:, fc * L + ig * 512: fc * L + ig * 512 + 512],
                     start=(fc == 0), stop=(fc == NCHUNK - 1))
            p.stt(ychunk[:, ig * 512:(ig + 1) * 512], pa[:], bT[:, oc:oc + 1],
                  xchunk[:, xo + ig * 512: xo + ig * 512 + 512], ALU.add, ALU.add)
        p.act(ybf[:, oc * L:(oc + 1) * L], ychunk[:, 0:L])
        p.dma(yscr_hbm[oc * 128:(oc + 1) * 128, :], ychunk[:, 0:L], eng="gpsimd")


L0_ATTN_INPUTS = [("Wqkv", [D, 3 * D], F32), ("bqkvT", [128, 24], F32),
                  ("bvc", [64, 16], F32), ("Wout", [D, D], F32),
                  ("boutT", [128, 8], F32), ("relTc", [128, L], F32)]


def emit_l0_attn(nc, p, t):
    import contextlib
    xT, Wqkv, bqkvT, bvc, Wout, boutT, relTc, onesc, fcc, xout, yscr = (
        t["xT"], t["Wqkv"], t["bqkvT"], t["bvc"], t["Wout"], t["boutT"],
        t["relTc"], t["onesc"], t["fcc"], t["xout"], t["yscr"])
    with contextlib.ExitStack() as es:
        fc = es.enter_context(nc.sbuf_tensor("fc", [128, 8], F32))
        xbf = es.enter_context(nc.sbuf_tensor("xbf", [128, NCHUNK * L], BF16))
        Vt = es.enter_context(nc.sbuf_tensor("Vt", [128, NIC * 1040], BF16))
        obf = es.enter_context(nc.sbuf_tensor("obf", [128, NCHUNK * L], BF16))
        relT = es.enter_context(nc.sbuf_tensor("relT", [128, L], F32))
        ones_bf = es.enter_context(nc.sbuf_tensor("ones_bf", [128, 128], BF16))
        bqkvT_sb = es.enter_context(nc.sbuf_tensor("bqkvT_sb", [128, 24], F32))
        bv_sb = es.enter_context(nc.sbuf_tensor("bv_sb", [64, 16], F32))
        boutT_sb = es.enter_context(nc.sbuf_tensor("boutT_sb", [128, 8], F32))
        scr_f32 = es.enter_context(nc.sbuf_tensor("scr_f32", [128, 3 * D], F32))
        ps_a = es.enter_context(nc.psum_tensor("ps_a", [128, 512], F32))
        ps_a2 = es.enter_context(nc.psum_tensor("ps_a2", [128, 512], F32))
        ps_o = es.enter_context(nc.psum_tensor("ps_o", [65, 512], F32))
        ps_o2 = es.enter_context(nc.psum_tensor("ps_o2", [65, 512], F32))
        ps_f = es.enter_context(nc.psum_tensor("ps_f", [64, 512], F32))
        ps_r1 = es.enter_context(nc.psum_tensor("ps_r1", [1, 512], F32))
        psAB = [ps_a, ps_a2]

        p.dma(fc[:], fcc)
        p.dma(ones_bf[:], onesc)
        p.dma(relT[:], relTc)
        p.dma(bqkvT_sb[:], bqkvT)
        p.dma(bv_sb[:], bvc)
        p.dma(boutT_sb[:], boutT)
        load_xbf(p, nc, xT, xbf, scr_f32)

        with contextlib.ExitStack() as es2:
            wqkv_bf = es2.enter_context(nc.sbuf_tensor("wqkv_bf", [128, NCHUNK * 3 * D], BF16))
            qxA = es2.enter_context(nc.sbuf_tensor("qxA", [65, L], BF16))
            qxB = es2.enter_context(nc.sbuf_tensor("qxB", [65, L], BF16))
            kxA = es2.enter_context(nc.sbuf_tensor("kxA", [65, L], BF16))
            kxB = es2.enter_context(nc.sbuf_tensor("kxB", [65, L], BF16))
            kf32 = es2.enter_context(nc.sbuf_tensor("kf32", [128, 512], F32))
            sqh = es2.enter_context(nc.sbuf_tensor("sqh", [64, 512], BF16))
            nq_row = es2.enter_context(nc.sbuf_tensor("nq_row", [1, 512], F32))
            km = es2.enter_context(nc.sbuf_tensor("km", [1, 5], F32))
            rcp = es2.enter_context(nc.sbuf_tensor("rcp", [1, 512], F32))
            rcpb = es2.enter_context(nc.sbuf_tensor("rcpb", [1, 512], BF16))
            Fsb = es2.enter_context(nc.sbuf_tensor("Fsb", [64, 512], F32))
            tmul = es2.enter_context(nc.sbuf_tensor("tmul", [64, 512], F32))
            PT = es2.enter_context(nc.sbuf_tensor("PT", [128, 512], BF16))
            PT2 = es2.enter_context(nc.sbuf_tensor("PT2", [128, 512], BF16))

            load_weight_bf16(p, nc, Wqkv, wqkv_bf, scr_f32, 3 * D, NCHUNK)

            p.memset(Vt.rearrange("p (j c) -> p j c", c=65)[:, :, 64:65], 1.0)
            for ic in range(NIC):
                for hg in range(2):
                    pa = psAB[(2 * ic + hg) % 2]
                    for dc in range(NCHUNK):
                        p.mm(pa[:], xbf[:, dc * L + ic * 128: dc * L + ic * 128 + 128],
                             wqkv_bf[:, dc * 3 * D + 2 * D + hg * 512: dc * 3 * D + 2 * D + hg * 512 + 512],
                             start=(dc == 0), stop=(dc == NCHUNK - 1))
                    dst = Vt[:, ic * 1040 + hg * 520: ic * 1040 + hg * 520 + 520]
                    dst3 = dst.rearrange("p (h c) -> p h c", c=65)[:, :, 0:64]
                    src3 = pa.rearrange("p (h c) -> p h c", c=64)
                    p.vcopy(dst3, src3)

            def build_qk(p, hp):
                for ig in range(NIG):
                    pa = psAB[ig % 2]
                    for dc in range(NCHUNK):
                        p.mm(pa[:], wqkv_bf[:, dc * 3 * D + hp * 128: dc * 3 * D + hp * 128 + 128],
                             xbf[:, dc * L + ig * 512: dc * L + ig * 512 + 512],
                             start=(dc == 0), stop=(dc == NCHUNK - 1))
                    p.act(qxA[0:64, ig * 512:(ig + 1) * 512], pa[0:64, :],
                          bias=bqkvT_sb[0:64, hp:hp + 1])
                    p.act(qxB[0:64, ig * 512:(ig + 1) * 512], pa[64:128, :],
                          bias=bqkvT_sb[64:128, hp:hp + 1])
                for ig in range(NIG):
                    pa = psAB[ig % 2]
                    for dc in range(NCHUNK):
                        p.mm(pa[:], wqkv_bf[:, dc * 3 * D + D + hp * 128: dc * 3 * D + D + hp * 128 + 128],
                             xbf[:, dc * L + ig * 512: dc * L + ig * 512 + 512],
                             start=(dc == 0), stop=(dc == NCHUNK - 1))
                    p.act(kf32[:, 0:512], pa[:], bias=bqkvT_sb[:, 8 + hp: 9 + hp])
                    p.tt(kxA[0:64, ig * 512:(ig + 1) * 512], kf32[0:64, 0:512],
                         relT[0:64, ig * 512:(ig + 1) * 512], ALU.add)
                    p.tt(kxB[0:64, ig * 512:(ig + 1) * 512], kf32[64:128, 0:512],
                         relT[64:128, ig * 512:(ig + 1) * 512], ALU.add,
                         eng="gpsimd")

            def bv(hp, hh):
                return bv_sb[:, 2 * hp + hh: 2 * hp + hh + 1]

            env = dict(Vt=Vt, obf=obf, qxA=qxA, qxB=qxB, kxA=kxA, kxB=kxB,
                       sqh=sqh, nq_row=nq_row, km=km, rcp=rcp,
                       rcpb=rcpb, Fsb=Fsb, tmul=tmul, PT=PT, PT2=PT2,
                       ps_a=ps_a, ps_a2=ps_a2,
                       ps_o=ps_o, ps_o2=ps_o2, ps_f=ps_f, ps_r1=ps_r1, ones_bf=ones_bf,
                       build_qk=build_qk, bv=bv, fc=fc)
            emit_attention(p, nc, env, layer=0)
        p.barrier()

        with contextlib.ExitStack() as es3:
            wout_bf = es3.enter_context(nc.sbuf_tensor("wout_bf", [128, NCHUNK * D], BF16))
            xchunk = es3.enter_context(nc.sbuf_tensor("xchunk", [128, L], F32))
            ychunk = es3.enter_context(nc.sbuf_tensor("ychunk", [128, L], F32))
            ln = alloc_ln_tiles(es3, nc, chunk_scr2=ychunk)
            load_weight_bf16(p, nc, Wout, wout_bf, scr_f32, D, NCHUNK)
            env2 = dict(obf=obf, ps_a=ps_a, ps_a2=ps_a2, xchunk=xchunk, ychunk=ychunk)
            ybf = xbf
            emit_proj_residual(p, nc, env2, wout_bf, boutT_sb, xT, yscr, ybf)
            emit_postln(p, nc, yscr, ybf, ones_bf, xout, ln, fc)


L0_FFN_INPUTS = [("WT", [D, D], F32), ("bT", [128, 8], F32),
                 ("ldT", [100, 4], F32), ("luT", [4, 50], F32),
                 ("dn", [100, D], F32), ("uaT", [50, D], F32)]


def emit_l0_ffn(nc, p, t):
    import contextlib
    xT, WT, bT, ldT, luT, dn, uaT, onesc, fcc, xout, yscr = (
        t["xT"], t["WT"], t["bT"], t["ldT"], t["luT"], t["dn"], t["uaT"],
        t["onesc"], t["fcc"], t["xout"], t["yscr"])
    with contextlib.ExitStack() as es:
        fc = es.enter_context(nc.sbuf_tensor("fc", [128, 8], F32))
        xbf = es.enter_context(nc.sbuf_tensor("xbf", [128, NCHUNK * L], BF16))
        ones_bf = es.enter_context(nc.sbuf_tensor("ones_bf", [128, 128], BF16))
        bT_sb = es.enter_context(nc.sbuf_tensor("bT_sb", [128, 8], F32))
        scr_f32 = es.enter_context(nc.sbuf_tensor("scr_f32", [128, 2048], F32))
        weff = es.enter_context(nc.sbuf_tensor("weff", [128, NCHUNK * D], BF16))
        ldT_sb = es.enter_context(nc.sbuf_tensor("ldT_sb", [100, 4], F32))
        luT_sb = es.enter_context(nc.sbuf_tensor("luT_sb", [4, 50], F32))
        dn_sb = es.enter_context(nc.sbuf_tensor("dn_sb", [100, D], F32))
        uaT_sb = es.enter_context(nc.sbuf_tensor("uaT_sb", [50, D], F32))
        D2_sb = es.enter_context(nc.sbuf_tensor("D2_sb", [4, D], F32))
        B_sb = es.enter_context(nc.sbuf_tensor("B_sb", [50, D], F32))
        xchunk = es.enter_context(nc.sbuf_tensor("xchunk", [128, 2 * L], F32))
        ychunk = es.enter_context(nc.sbuf_tensor("ychunk", [128, L], F32))
        ps_a = es.enter_context(nc.psum_tensor("ps_a", [128, 512], F32))
        ps_a2 = es.enter_context(nc.psum_tensor("ps_a2", [128, 512], F32))
        psAB = [ps_a, ps_a2]
        ps_b = es.enter_context(nc.psum_tensor("ps_b", [50, 512], F32))
        ln = alloc_ln_tiles(es, nc)

        p.dma(fc[:], fcc)
        p.dma(ones_bf[:], onesc)
        p.dma(bT_sb[:], bT)
        p.dma(ldT_sb[:], ldT)
        p.dma(luT_sb[:], luT)
        p.dma(dn_sb[:], dn)
        p.dma(uaT_sb[:], uaT)
        load_xbf(p, nc, xT, xbf, scr_f32)

        for g in range(2):
            p.mm(ps_b[0:4, :], ldT_sb[:], dn_sb[:, g * 512:(g + 1) * 512])
            p.vcopy(D2_sb[:, g * 512:(g + 1) * 512], ps_b[0:4, :])
        for g in range(2):
            p.mm(ps_b[:], luT_sb[:], D2_sb[:, g * 512:(g + 1) * 512])
            p.vcopy(B_sb[:, g * 512:(g + 1) * 512], ps_b[:])
        for c in range(NCHUNK):
            so = (c % 2) * D
            p.dma(scr_f32[:, so:so + D], WT[c * 128:(c + 1) * 128, :])
            for g in range(2):
                pa = psAB[g]
                p.mm(pa[:], B_sb[:, c * 128:(c + 1) * 128], uaT_sb[:, g * 512:(g + 1) * 512])
                p.tt(weff[:, c * D + g * 512: c * D + g * 512 + 512], pa[:],
                     scr_f32[:, so + g * 512: so + g * 512 + 512], ALU.add)
        ybf = xbf
        for oc in range(NCHUNK):
            xo = (oc % 2) * L
            p.dma(xchunk[:, xo:xo + L], xT[oc * 128:(oc + 1) * 128, :])
            for ig in range(NIG):
                pa = psAB[ig % 2]
                for ic in range(NCHUNK):
                    p.mm(pa[:], weff[:, ic * D + oc * 128: ic * D + oc * 128 + 128],
                         xbf[:, ic * L + ig * 512: ic * L + ig * 512 + 512],
                         start=(ic == 0), stop=(ic == NCHUNK - 1))
                p.stt(ychunk[:, ig * 512:(ig + 1) * 512], pa[:], bT_sb[:, oc:oc + 1],
                      xchunk[:, xo + ig * 512: xo + ig * 512 + 512], ALU.add, ALU.add)
            p.dma(yscr[oc * 128:(oc + 1) * 128, :], ychunk[:, 0:L], eng="gpsimd")
        load_xbf(p, nc, yscr, ybf, ychunk)
        emit_postln(p, nc, yscr, ybf, ones_bf, xout, ln, fc)


L1_ATTN_INPUTS = [("WqT", [D, D], F32), ("WkT", [D, D], F32),
                  ("WvT", [D, D], F32), ("WoT", [D, D], F32),
                  ("boT", [128, 8], F32), ("qg", [64, 1], F32),
                  ("qb", [64, 1], F32), ("kg", [64, 1], F32),
                  ("kb", [64, 1], F32), ("cosc", [64, L], BF16),
                  ("sinc", [64, L], BF16), ("rotc", [64, 64], BF16),
                  ("tric", [128, 128], F32)]


def emit_l1_attn(nc, p, t):
    import contextlib
    (xT, WqT, WkT, WvT, WoT, boT, qg, qb, kg, kb, cosc, sinc, rotc, tric,
     onesc, fcc, xout, yscr) = (
        t["xT"], t["WqT"], t["WkT"], t["WvT"], t["WoT"], t["boT"], t["qg"],
        t["qb"], t["kg"], t["kb"], t["cosc"], t["sinc"], t["rotc"], t["tric"],
        t["onesc"], t["fcc"], t["xout"], t["yscr"])
    with contextlib.ExitStack() as es:
        fc = es.enter_context(nc.sbuf_tensor("fc", [128, 8], F32))
        xbf = es.enter_context(nc.sbuf_tensor("xbf", [128, NCHUNK * L], BF16))
        Vt = es.enter_context(nc.sbuf_tensor("Vt", [128, NIC * 1040], BF16))
        obf = es.enter_context(nc.sbuf_tensor("obf", [128, NCHUNK * L], BF16))
        ones_bf = es.enter_context(nc.sbuf_tensor("ones_bf", [128, 128], BF16))
        cosT = es.enter_context(nc.sbuf_tensor("cosT", [64, L], BF16))
        sinT = es.enter_context(nc.sbuf_tensor("sinT", [64, L], BF16))
        rotT = es.enter_context(nc.sbuf_tensor("rotT", [64, 64], BF16))
        tri = es.enter_context(nc.sbuf_tensor("tri", [128, 128], F32))
        gb_sb = es.enter_context(nc.sbuf_tensor("gb_sb", [64, 4], F32))
        boT_sb = es.enter_context(nc.sbuf_tensor("boT_sb", [128, 8], F32))
        scr_f32 = es.enter_context(nc.sbuf_tensor("scr_f32", [128, 2048], F32))
        ps_a = es.enter_context(nc.psum_tensor("ps_a", [128, 512], F32))
        ps_a2 = es.enter_context(nc.psum_tensor("ps_a2", [128, 512], F32))
        psAB = [ps_a, ps_a2]
        ps_o = es.enter_context(nc.psum_tensor("ps_o", [65, 512], F32))
        ps_o2 = es.enter_context(nc.psum_tensor("ps_o2", [65, 512], F32))
        ps_f = es.enter_context(nc.psum_tensor("ps_f", [64, 512], F32))
        ps_r1 = es.enter_context(nc.psum_tensor("ps_r1", [1, 512], F32))

        p.dma(fc[:], fcc)
        p.dma(ones_bf[:], onesc)
        p.dma(cosT[:], cosc)
        p.dma(sinT[:], sinc)
        p.dma(rotT[:], rotc)
        p.dma(tri[:], tric)
        p.dma(gb_sb[:, 0:1], qg)
        p.dma(gb_sb[:, 1:2], qb)
        p.dma(gb_sb[:, 2:3], kg)
        p.dma(gb_sb[:, 3:4], kb)
        p.dma(boT_sb[:], boT)
        load_xbf(p, nc, xT, xbf, scr_f32)

        with contextlib.ExitStack() as esv:
            wvT_bf = esv.enter_context(nc.sbuf_tensor("wvT_bf", [128, NCHUNK * D], BF16))
            load_weight_bf16(p, nc, WvT, wvT_bf, scr_f32, D, NCHUNK)
            p.memset(Vt.rearrange("p (j c) -> p j c", c=65)[:, :, 64:65], 1.0)
            for ic in range(NIC):
                for hg in range(2):
                    pa = psAB[(2 * ic + hg) % 2]
                    for dc in range(NCHUNK):
                        p.mm(pa[:], xbf[:, dc * L + ic * 128: dc * L + ic * 128 + 128],
                             wvT_bf[:, dc * D + hg * 512: dc * D + hg * 512 + 512],
                             start=(dc == 0), stop=(dc == NCHUNK - 1))
                    dst = Vt[:, ic * 1040 + hg * 520: ic * 1040 + hg * 520 + 520]
                    dst3 = dst.rearrange("p (h c) -> p h c", c=65)[:, :, 0:64]
                    src3 = pa.rearrange("p (h c) -> p h c", c=64)
                    p.vcopy(dst3, src3)
        p.barrier()

        with contextlib.ExitStack() as es2:
            wqT_bf = es2.enter_context(nc.sbuf_tensor("wqT_bf", [128, NCHUNK * D], BF16))
            wkT_bf = es2.enter_context(nc.sbuf_tensor("wkT_bf", [128, NCHUNK * D], BF16))
            qxA = es2.enter_context(nc.sbuf_tensor("qxA", [65, L], BF16))
            qxB = es2.enter_context(nc.sbuf_tensor("qxB", [65, L], BF16))
            kxA = es2.enter_context(nc.sbuf_tensor("kxA", [65, L], BF16))
            kxB = es2.enter_context(nc.sbuf_tensor("kxB", [65, L], BF16))
            qfA = es2.enter_context(nc.sbuf_tensor("qfA", [64, 512], F32))
            qfB = es2.enter_context(nc.sbuf_tensor("qfB", [64, 512], F32))
            hbf = es2.enter_context(nc.sbuf_tensor("hbf", [64, 512], BF16))
            sqh = es2.enter_context(nc.sbuf_tensor("sqh", [64, 512], BF16))
            st_s = es2.enter_context(nc.sbuf_tensor("st_s", [1, 512], F32))
            st_q = es2.enter_context(nc.sbuf_tensor("st_q", [1, 512], F32))
            st_a = es2.enter_context(nc.sbuf_tensor("st_a", [1, 512], F32))
            rs_bf = es2.enter_context(nc.sbuf_tensor("rs_bf", [1, 512], BF16))
            mr_bf = es2.enter_context(nc.sbuf_tensor("mr_bf", [1, 512], BF16))
            Brstd = es2.enter_context(nc.sbuf_tensor("Brstd_h", [64, 512], F32))
            Bmr = es2.enter_context(nc.sbuf_tensor("Bmr_h", [64, 512], F32))
            t1 = es2.enter_context(nc.sbuf_tensor("t1", [64, 512], F32))
            t2 = es2.enter_context(nc.sbuf_tensor("t2", [64, 512], F32))
            qln = es2.enter_context(nc.sbuf_tensor("qln", [64, 512], BF16))
            nq_row = es2.enter_context(nc.sbuf_tensor("nq_row", [1, 512], F32))
            km = es2.enter_context(nc.sbuf_tensor("km", [1, 5], F32))
            rcp = es2.enter_context(nc.sbuf_tensor("rcp", [1, 512], F32))
            rcpb = es2.enter_context(nc.sbuf_tensor("rcpb", [1, 512], BF16))
            Fsb = es2.enter_context(nc.sbuf_tensor("Fsb", [64, 512], F32))
            tmul = es2.enter_context(nc.sbuf_tensor("tmul", [64, 512], F32))
            PT = es2.enter_context(nc.sbuf_tensor("PT", [128, 512], BF16))
            PT2 = es2.enter_context(nc.sbuf_tensor("PT2", [128, 512], BF16))

            load_weight_bf16(p, nc, WqT, wqT_bf, scr_f32, D, NCHUNK)
            load_weight_bf16(p, nc, WkT, wkT_bf, scr_f32, D, NCHUNK)

            def lnhead(p, hp, w_bf, gcol, bcol, xdstA, xdstB):
                for ig in range(NIG):
                    sl = slice(ig * 512, (ig + 1) * 512)
                    pa = psAB[ig % 2]
                    for dc in range(NCHUNK):
                        p.mm(pa[:], w_bf[:, dc * D + hp * 128: dc * D + hp * 128 + 128],
                             xbf[:, dc * L + ig * 512: dc * L + ig * 512 + 512],
                             start=(dc == 0), stop=(dc == NCHUNK - 1))
                    p.act(qfA[:], pa[0:64, :])
                    p.act(qfB[:], pa[64:128, :])
                    for hh, xdst in ((0, xdstA), (1, xdstB)):
                        rows = qfA[:] if hh == 0 else qfB[:]
                        p.vcopy(hbf[:], rows, eng="gpsimd")
                        p.mm(ps_r1[:], ones_bf[0:64, 0:1], hbf[:])
                        p.act(st_s[:], ps_r1[:])
                        p.act(sqh[:, 0:512], hbf[:], func=AF.Square)
                        p.mm(ps_r1[:], ones_bf[0:64, 0:1], sqh[:, 0:512])
                        p.act(st_q[:], ps_r1[:])
                        p.tsc(st_s[:], st_s[:], 1.0 / DH, None, ALU.mult)
                        p.tsc(st_q[:], st_q[:], 1.0 / DH, None, ALU.mult)
                        p.tt(st_a[:], st_s[:], st_s[:], ALU.mult)
                        p.tt(st_q[:], st_q[:], st_a[:], ALU.subtract)
                        p.act(st_q[:], st_q[:], func=AF.Sqrt, bias=fc[0:1, 3:4])
                        p.recip(st_a[:], st_q[:])
                        p.tt(st_q[:], st_s[:], st_a[:], ALU.mult)
                        p.act(rs_bf[:], st_a[:])
                        p.act(mr_bf[:], st_q[:])
                        p.mm(ps_f[:], ones_bf[0:1, 0:64], rs_bf[:])
                        p.act(Brstd[:], ps_f[:])
                        p.mm(ps_f[:], ones_bf[0:1, 0:64], mr_bf[:])
                        p.act(Bmr[:], ps_f[:])
                        p.tt(t1[:], rows, Brstd[:], ALU.mult)
                        p.tt(t1[:], t1[:], Bmr[:], ALU.subtract)
                        p.act(qln[:], t1[:], bias=gb_sb[:, bcol:bcol + 1],
                              scale=gb_sb[:, gcol:gcol + 1])
                        p.mm(ps_f[:], rotT[:], qln[:])
                        p.tt(t1[:], qln[:], cosT[:, sl], ALU.mult)
                        p.tt(t2[:], ps_f[:], sinT[:, sl], ALU.mult)
                        p.tt(xdst[0:64, sl], t1[:], t2[:], ALU.add)

            def build_qk(p, hp):
                lnhead(p, hp, wqT_bf, 0, 1, qxA, qxB)
                lnhead(p, hp, wkT_bf, 2, 3, kxA, kxB)

            def bv(hp, hh):
                return None

            env = dict(Vt=Vt, obf=obf, qxA=qxA, qxB=qxB, kxA=kxA, kxB=kxB,
                       sqh=sqh, nq_row=nq_row, km=km, rcp=rcp,
                       rcpb=rcpb, Fsb=Fsb, tmul=tmul, PT=PT, PT2=PT2,
                       ps_a=ps_a, ps_a2=ps_a2,
                       ps_o=ps_o, ps_o2=ps_o2, ps_f=ps_f, ps_r1=ps_r1, ones_bf=ones_bf,
                       build_qk=build_qk, bv=bv, tri=tri, fc=fc)
            emit_attention(p, nc, env, layer=1)
        p.barrier()

        with contextlib.ExitStack() as es3:
            woT_bf = es3.enter_context(nc.sbuf_tensor("woT_bf", [128, NCHUNK * D], BF16))
            xchunk = es3.enter_context(nc.sbuf_tensor("xchunk", [128, 2 * L], F32))
            ychunk = es3.enter_context(nc.sbuf_tensor("ychunk", [128, L], F32))
            ln = alloc_ln_tiles(es3, nc, chunk_scr2=ychunk)
            load_weight_bf16(p, nc, WoT, woT_bf, scr_f32, D, NCHUNK)
            env2 = dict(obf=obf, ps_a=ps_a, ps_a2=ps_a2, xchunk=xchunk, ychunk=ychunk)
            ybf = xbf
            emit_proj_residual(p, nc, env2, woT_bf, boT_sb, xT, yscr, ybf)
            emit_postln(p, nc, yscr, ybf, ones_bf, xout, ln, fc)


L1_FFN_INPUTS = [("k0t", [D, 4096], U8), ("k0b", [D, 4096], U8),
                 ("k1", [4096, 4096], U8), ("k2", [4096, D], U8),
                 ("s0", [1, 1], F32), ("s1", [1, 1], F32), ("s2", [1, 1], F32)]


def emit_l1_ffn(nc, p, t):
    import contextlib
    (xT, k0t, k0b, k1, k2, s0, s1, s2, onesc, onesf, fcc, xout, yscr,
     a2scr) = (
        t["xT"], t["k0t"], t["k0b"], t["k1"], t["k2"], t["s0"], t["s1"],
        t["s2"], t["onesc"], t["onesf"], t["fcc"], t["xout"], t["yscr"],
        t["a2scr"])
    with contextlib.ExitStack() as es:
        fc = es.enter_context(nc.sbuf_tensor("fc", [128, 8], F32))
        xbf = es.enter_context(nc.sbuf_tensor("xbf", [128, NCHUNK * L], BF16))
        ones_bf = es.enter_context(nc.sbuf_tensor("ones_bf", [128, 128], BF16))
        ones_f = es.enter_context(nc.sbuf_tensor("ones_f", [1, 128], F32))
        s_sb = es.enter_context(nc.sbuf_tensor("s_sb", [1, 3], F32))
        sb_b = es.enter_context(nc.sbuf_tensor("sb_b", [128, 3], F32))
        scr_f32 = es.enter_context(nc.sbuf_tensor("scr_f32", [128, 2048], F32))
        ps_a = es.enter_context(nc.psum_tensor("ps_a", [128, 512], F32))
        ps_a2 = es.enter_context(nc.psum_tensor("ps_a2", [128, 512], F32))
        psAB = [ps_a, ps_a2]
        ps_c = es.enter_context(nc.psum_tensor("ps_cb", [128, 1], F32))

        p.dma(fc[:], fcc)
        p.dma(ones_bf[:], onesc)
        p.dma(ones_f[:], onesf)
        p.dma(s_sb[:, 0:1], s0)
        p.dma(s_sb[:, 1:2], s1)
        p.dma(s_sb[:, 2:3], s2)
        for j, sc in ((0, 7), (1, 1), (2, None)):
            p.mm(ps_c[:], ones_f[:], s_sb[:, j:j + 1])
            if sc is not None:
                p.act(sb_b[:, j:j + 1], ps_c[:], scale=fc[:, sc:sc + 1])
            else:
                p.act(sb_b[:, j:j + 1], ps_c[:])
        load_xbf(p, nc, xT, xbf, scr_f32)

        with contextlib.ExitStack() as esA:
            a1 = esA.enter_context(nc.sbuf_tensor("a1", [128, 32 * L], BF16))
            ka = esA.enter_context(nc.sbuf_tensor("ka", [128, 256], U8))
            kb_ = esA.enter_context(nc.sbuf_tensor("kb_", [128, 256], U8))
            ta = esA.enter_context(nc.sbuf_tensor("ta", [128, 256], BF16))
            tb = esA.enter_context(nc.sbuf_tensor("tb", [128, 256], BF16))
            wf = esA.enter_context(nc.sbuf_tensor("wf", [128, 2 * NCHUNK * 128], BF16))
            for f1c in range(32):
                wo = (f1c % 2) * NCHUNK * 128
                for ic in range(NCHUNK):
                    o = (ic % 2) * 128
                    p.dma(ka[:, o:o + 128], k0t[ic * 128:(ic + 1) * 128, f1c * 128:(f1c + 1) * 128])
                    p.dma(kb_[:, o:o + 128], k0b[ic * 128:(ic + 1) * 128, f1c * 128:(f1c + 1) * 128])
                    p.vcopy(ta[:, o:o + 128], ka[:, o:o + 128], eng="gpsimd")
                    p.vcopy(tb[:, o:o + 128], kb_[:, o:o + 128], eng="gpsimd")
                    p.tt(wf[:, wo + ic * 128: wo + ic * 128 + 128],
                         ta[:, o:o + 128], tb[:, o:o + 128], ALU.subtract)
                for ig in range(NIG):
                    pa = psAB[ig % 2]
                    for ic in range(NCHUNK):
                        p.mm(pa[:], wf[:, wo + ic * 128: wo + ic * 128 + 128],
                             xbf[:, ic * L + ig * 512: ic * L + ig * 512 + 512],
                             start=(ic == 0), stop=(ic == NCHUNK - 1))
                    p.act(a1[:, f1c * L + ig * 512: f1c * L + ig * 512 + 512], pa[:],
                          func=AF.Relu, scale=sb_b[:, 0:1])

            with contextlib.ExitStack() as esB:
                k1s = esB.enter_context(nc.sbuf_tensor("k1s", [128, 4096], U8))
                w1 = esB.enter_context(nc.sbuf_tensor("w1", [128, 4096], BF16))
                a2c = esB.enter_context(nc.sbuf_tensor("a2c", [128, L], BF16))
                for f2c in range(32):
                    for q in range(4):
                        src = k1[q * 1024:(q + 1) * 1024, f2c * 128:(f2c + 1) * 128]
                        src3 = src.rearrange("(c p) f -> p c f", p=128)
                        dst3 = k1s[:, q * 1024:(q + 1) * 1024].rearrange("p (c f) -> p c f", f=128)
                        p.dma(dst3, src3)
                    for hlf in range(2):
                        hs = slice(hlf * 2048, hlf * 2048 + 2048)
                        p.tsc(w1[:, hs], k1s[:, hs], 2.0, -1.0, ALU.mult, ALU.add,
                              eng="gpsimd")
                    for ig in range(NIG):
                        pa = psAB[ig % 2]
                        for ic in range(32):
                            p.mm(pa[:], w1[:, ic * 128:(ic + 1) * 128],
                                 a1[:, ic * L + ig * 512: ic * L + ig * 512 + 512],
                                 start=(ic == 0), stop=(ic == 31))
                        p.act(a2c[:, ig * 512:(ig + 1) * 512], pa[:],
                              func=AF.Relu, scale=sb_b[:, 1:2])
                    p.dma(a2scr[f2c * 128:(f2c + 1) * 128, :], a2c[:], eng="gpsimd")
        p.barrier()

        with contextlib.ExitStack() as esC:
            w2 = esC.enter_context(nc.sbuf_tensor("w2", [128, 32 * D], BF16))
            k2s = esC.enter_context(nc.sbuf_tensor("k2s", [128, D], U8))
            a2g = esC.enter_context(nc.sbuf_tensor("a2g", [128, 32 * 512], BF16))
            xchunk = esC.enter_context(nc.sbuf_tensor("xchunk", [128, 1024], F32))
            tch = esC.enter_context(nc.sbuf_tensor("tch", [128, 1024], F32))
            ybf = xbf
            for fc2 in range(32):
                p.dma(k2s[:], k2[fc2 * 128:(fc2 + 1) * 128, :])
                p.tsc(w2[:, fc2 * D:(fc2 + 1) * D], k2s[:],
                      2.0, -1.0, ALU.mult, ALU.add, eng="gpsimd")
            for ig in range(NIG):
                for c2 in range(32):
                    p.dma(a2g[:, c2 * 512:(c2 + 1) * 512],
                          a2scr[c2 * 128:(c2 + 1) * 128, ig * 512:(ig + 1) * 512])
                for oc in range(NCHUNK):
                    pa = psAB[oc % 2]
                    o = (oc % 2) * 512
                    for c2 in range(32):
                        p.mm(pa[:], w2[:, c2 * D + oc * 128: c2 * D + oc * 128 + 128],
                             a2g[:, c2 * 512:(c2 + 1) * 512],
                             start=(c2 == 0), stop=(c2 == 31))
                    p.dma(xchunk[:, o:o + 512], xT[oc * 128:(oc + 1) * 128, ig * 512:(ig + 1) * 512])
                    p.act(tch[:, o:o + 512], pa[:], scale=sb_b[:, 2:3])
                    p.tt(tch[:, o:o + 512], tch[:, o:o + 512], xchunk[:, o:o + 512], ALU.add)
                    p.vcopy(ybf[:, oc * L + ig * 512: oc * L + ig * 512 + 512],
                            tch[:, o:o + 512], eng="gpsimd")
                    p.dma(yscr[oc * 128:(oc + 1) * 128, ig * 512:(ig + 1) * 512],
                          tch[:, o:o + 512], eng="gpsimd")
            with contextlib.ExitStack() as esL:
                ln = alloc_ln_tiles(esL, nc)
                emit_postln(p, nc, yscr, ybf, ones_bf, xout, ln, fc)


PHASES = [("p1_", emit_l0_attn, L0_ATTN_INPUTS),
          ("p2_", emit_l0_ffn, L0_FFN_INPUTS),
          ("p3_", emit_l1_attn, L1_ATTN_INPUTS),
          ("p4_", emit_l1_ffn, L1_FFN_INPUTS)]


def build_fused():
    nc = bass.Bass("TRN2", target_bir_lowering=False, debug=False)
    onesc = nc.dram_tensor("onesc", [128, 128], BF16, kind="ExternalInput").ap()
    onesf = nc.dram_tensor("onesf", [1, 128], F32, kind="ExternalInput").ap()
    fcc = nc.dram_tensor("fcc", [128, 8], F32, kind="ExternalInput").ap()
    xin = nc.dram_tensor("xT", [D, L], F32, kind="ExternalInput").ap()
    xout = nc.dram_tensor("xout", [D, L], F32, kind="ExternalOutput").ap()
    yscr = nc.dram_tensor("yscr", [D, L], F32, kind="Internal").ap()
    a2scr = nc.dram_tensor("a2scr", [4096, L], BF16, kind="Internal").ap()
    xmid = [nc.dram_tensor(f"xmid{i}", [D, L], F32, kind="Internal").ap()
            for i in range(3)]
    p = P(nc)
    chain = [xin, xmid[0], xmid[1], xmid[2], xout]
    for i, (pre, emit, specs) in enumerate(PHASES):
        t = {"onesc": onesc, "onesf": onesf, "fcc": fcc, "yscr": yscr,
             "a2scr": a2scr, "xT": chain[i], "xout": chain[i + 1]}
        for (nm, shp, dt) in specs:
            t[nm] = nc.dram_tensor(pre + nm, shp, dt, kind="ExternalInput").ap()
        emit(NCP(nc, pre), p, t)
        p.barrier()
    p.build()
    return nc


def _phase_arrays(inp):
    f32 = np.float32
    ones_bf, ones_f, cosT, sinT, rotT, tri, fcc = _consts()
    relT = np.zeros((128, L), dtype=f32)
    relT[0:64, 1:] = np.asarray(inp["L0_attn_rel_emb"])[0:L - 1].T
    relT[64:128] = relT[0:64]
    k0 = np.asarray(inp["L1_ffn_k0"]).view(np.uint8)
    com = {
        "onesc": ones_bf, "onesf": ones_f, "fcc": fcc,
        "Wqkv": np.ascontiguousarray(inp["L0_attn_Wqkv"], dtype=f32),
        "bqkvT": np.ascontiguousarray(np.asarray(inp["L0_attn_bqkv"], dtype=f32).reshape(24, 128).T),
        "bvc": np.ascontiguousarray(np.asarray(inp["L0_attn_bqkv"], dtype=f32)[2048:].reshape(16, 64).T),
        "Wout": np.ascontiguousarray(inp["L0_attn_Wout"], dtype=f32),
        "boutT": np.ascontiguousarray(np.asarray(inp["L0_attn_bout"], dtype=f32).reshape(8, 128).T),
        "relTc": relT,
        "WT": np.ascontiguousarray(np.asarray(inp["L0_ffn_W"], dtype=f32).T),
        "bT": np.ascontiguousarray(np.asarray(inp["L0_ffn_b"], dtype=f32).reshape(8, 128).T),
        "ldT": np.ascontiguousarray(np.asarray(inp["L0_ffn_lora_down"], dtype=f32).T),
        "luT": np.ascontiguousarray(np.asarray(inp["L0_ffn_lora_up"], dtype=f32).T),
        "dn": np.ascontiguousarray(inp["L0_ffn_down_aux"], dtype=f32),
        "uaT": np.ascontiguousarray(np.asarray(inp["L0_ffn_up_aux"], dtype=f32).T),
        "WqT": np.ascontiguousarray(np.asarray(inp["L1_attn_Wq"], dtype=f32).T),
        "WkT": np.ascontiguousarray(np.asarray(inp["L1_attn_Wk"], dtype=f32).T),
        "WvT": np.ascontiguousarray(np.asarray(inp["L1_attn_Wv"], dtype=f32).T),
        "WoT": np.ascontiguousarray(np.asarray(inp["L1_attn_Wo"], dtype=f32).T),
        "boT": np.ascontiguousarray(np.asarray(inp["L1_attn_bo"], dtype=f32).reshape(8, 128).T),
        "qg": np.asarray(inp["L1_attn_q_gamma"], dtype=f32).reshape(64, 1),
        "qb": np.asarray(inp["L1_attn_q_beta"], dtype=f32).reshape(64, 1),
        "kg": np.asarray(inp["L1_attn_k_gamma"], dtype=f32).reshape(64, 1),
        "kb": np.asarray(inp["L1_attn_k_beta"], dtype=f32).reshape(64, 1),
        "cosc": cosT, "sinc": sinT, "rotc": rotT, "tric": tri,
        "k0t": np.ascontiguousarray(k0[0:D]),
        "k0b": np.ascontiguousarray(k0[D:2 * D]),
        "k1": np.ascontiguousarray(np.asarray(inp["L1_ffn_k1"]).view(np.uint8)),
        "k2": np.ascontiguousarray(np.asarray(inp["L1_ffn_k2"]).view(np.uint8)),
        "s0": np.asarray(inp["L1_ffn_s0"], dtype=f32).reshape(1, 1),
        "s1": np.asarray(inp["L1_ffn_s1"], dtype=f32).reshape(1, 1),
        "s2": np.asarray(inp["L1_ffn_s2"], dtype=f32).reshape(1, 1),
    }
    return com


def _consts():
    fcc = np.tile(np.array([0.125, 2.0, -1.0, EPS, 1.0 / D, 1.0 / DH, 1.0, 4.0],
                           dtype=np.float32), (128, 1))
    ones_bf = np.ones((128, 128), dtype=BF16NP)
    ones_f = np.ones((1, 128), dtype=np.float32)
    pos = np.arange(L, dtype=np.float64)
    di = np.arange(0, DH, 2, dtype=np.float64) / DH
    theta = pos[None, :] * np.exp(-di[:, None] * np.log(10000.0))
    cosT = np.repeat(np.cos(theta), 2, axis=0).astype(BF16NP)
    sinT = np.repeat(np.sin(theta), 2, axis=0).astype(BF16NP)
    R = np.zeros((DH, DH), dtype=np.float32)
    for d in range(32):
        R[d, 2 * d + 1] = -1.0
    for d in range(32, 64):
        R[d, 2 * (d - 32)] = 1.0
    rotT = np.ascontiguousarray(R.T).astype(BF16NP)
    tri = np.where(np.arange(128)[:, None] <= np.arange(128)[None, :], 0.0, NEG).astype(np.float32)
    return ones_bf, ones_f, cosT, sinT, rotT, tri, fcc


def _get_nc(name, builder):
    if name not in _NC_CACHE:
        _NC_CACHE[name] = builder()
    return _NC_CACHE[name]


def kernel(**inp):
    n = 8
    f32 = np.float32
    com = _phase_arrays(inp)
    base = {"onesc": com["onesc"], "onesf": com["onesf"], "fcc": com["fcc"]}
    for (pre, _, specs) in PHASES:
        for (nm, _, _) in specs:
            base[pre + nm] = com[nm]
    nc = _get_nc("fused", build_fused)
    in_maps = []
    for b in range(n):
        m = dict(base)
        m["xT"] = np.ascontiguousarray(inp["L0_attn_x"][b].T).astype(f32, copy=False)
        in_maps.append(m)
    res = run_bass_kernel_spmd(nc, in_maps, core_ids=list(range(n)))
    out = np.stack([np.ascontiguousarray(r["xout"].T) for r in res.results], axis=0)
    return out.astype(np.float32, copy=False)


# version 11 (final)
# speedup vs baseline: 1.0187x; 1.0187x over previous
import numpy as np
import ml_dtypes
import concourse.bass as bass
import concourse.mybir as mybir
from concourse.bass_utils import run_bass_kernel_spmd

BF16NP = ml_dtypes.bfloat16
F32 = mybir.dt.float32
BF16 = mybir.dt.bfloat16
U8 = mybir.dt.uint8
AF = mybir.ActivationFunctionType
ALU = mybir.AluOpType
AX = mybir.AxisListType

L = 2048
D = 1024
H = 16
DH = 64
NCHUNK = D // 128
NIC = L // 128
NIG = L // 512
NEG = -1.0e9
EPS = 1e-5

_NC_CACHE = {}


def _ap_box(ap):
    lo = ap.offset
    hi = lo
    for (step, count) in ap.ap:
        hi += abs(step) * (count - 1)
    return (ap.tensor.name, lo, hi + 1)


class P:

    def __init__(self, nc, serial=False):
        self.nc = nc
        self.serial = serial
        self.steps = []

    def _add(self, eng, inc, fn, reads, writes):
        self.steps.append((eng, inc, fn, reads, writes))

    def barrier(self):
        self.steps.append(("BARRIER", 0, None, None, None))

    def dma(self, out, in_, eng="sync", **kw):
        self._add(eng, 16,
                  lambda e, out=out, in_=in_, kw=kw: e.dma_start(out=out, in_=in_, **kw),
                  [in_], [out])

    def mm(self, out, lhsT, rhs, start=True, stop=True):
        reads = [lhsT, rhs] + ([out] if not start else [])
        self._add("tensor", 1,
                  lambda e, out=out, l=lhsT, r=rhs, s=start, st=stop: e.matmul(out, l, r, start=s, stop=st),
                  reads, [out])

    def act(self, out, in_, func=AF.Copy, bias=0.0, scale=1.0):
        if func == AF.Copy and not isinstance(bias, float):
            func = AF.Identity
        reads = [in_] + [a for a in (bias, scale) if not isinstance(a, float)]
        self._add("scalar", 1,
                  lambda e, out=out, i=in_, f=func, b=bias, s=scale: e.activation(out=out, in_=i, func=f, bias=b, scale=s),
                  reads, [out])

    def tt(self, out, in0, in1, op, eng="vector"):
        self._add(eng, 1,
                  lambda e, out=out, a=in0, b=in1, op=op: e.tensor_tensor(out, a, b, op),
                  [in0, in1], [out])

    def stt(self, out, in0, scalar, in1, op0, op1, eng="vector"):
        self._add(eng, 1,
                  lambda e, out=out, a=in0, s=scalar, b=in1, o0=op0, o1=op1: e.scalar_tensor_tensor(out, a, s, b, o0, o1),
                  [in0, scalar, in1], [out])

    def tsc(self, out, in0, scalar1, scalar2, op0, op1=ALU.bypass, eng="vector"):
        reads = [in0] + [a for a in (scalar1, scalar2) if a is not None and not isinstance(a, float)]
        self._add(eng, 1,
                  lambda e, out=out, a=in0, s1=scalar1, s2=scalar2, o0=op0, o1=op1: e.tensor_scalar(out, a, s1, s2, o0, o1),
                  reads, [out])

    def vcopy(self, out, in_, eng="vector"):
        self._add(eng, 1, lambda e, out=out, i=in_: e.tensor_copy(out=out, in_=i),
                  [in_], [out])

    def recip(self, out, in_):
        self._add("vector", 1, lambda e, out=out, i=in_: e.reciprocal(out=out, in_=i),
                  [in_], [out])

    def rmax(self, out, in_):
        self._add("vector", 1, lambda e, out=out, i=in_: e.reduce_max(out, i, axis=AX.X),
                  [in_], [out])

    def memset(self, ap, val, eng="gpsimd"):
        self._add(eng, 1, lambda e, ap=ap, v=val: e.memset(ap, v), [], [ap])

    def build(self):
        nc = self.nc
        steps = self.steps
        n = len(steps)
        KSLOT = 8
        sem_of = [None] * n
        count_after = [0] * n
        sem_tot = {}
        dma_idx = {}
        for i, st in enumerate(steps):
            if st[0] == "BARRIER":
                continue
            eng, inc = st[0], st[1]
            if inc == 16:
                k = dma_idx.get(eng, 0)
                dma_idx[eng] = k + 1
                sid = ("d", eng, k % KSLOT)
            else:
                sid = ("c", eng)
            sem_of[i] = sid
            sem_tot[sid] = sem_tot.get(sid, 0) + inc
            count_after[i] = sem_tot[sid]

        deps = [None] * n
        writers = {}
        readers = {}
        last_dma = {}
        snap_at_barrier = []
        barrier_needs = {}
        pending_snapshot = None
        for i, st in enumerate(steps):
            if st[0] == "BARRIER":
                pending_snapshot = dict(sem_tot_run := {})
                snap = {}
                for j in range(i):
                    if sem_of[j] is not None:
                        s = sem_of[j]
                        if count_after[j] > snap.get(s, 0):
                            snap[s] = count_after[j]
                pending_snapshot = snap
                writers.clear()
                readers.clear()
                last_dma.clear()
                continue
            eng, inc, fn, rd, wr = st
            d = set()
            if inc == 16:
                sid = sem_of[i]
                if sid in last_dma:
                    d.add(last_dma[sid])
                last_dma[sid] = i
            for ap in rd:
                key, lo, hi = _ap_box(ap)
                for (j, l2, h2) in writers.get(key, ()):
                    if l2 < hi and lo < h2:
                        d.add(j)
            for ap in wr:
                key, lo, hi = _ap_box(ap)
                for (j, l2, h2) in writers.get(key, ()):
                    if l2 < hi and lo < h2:
                        d.add(j)
                for (j, l2, h2) in readers.get(key, ()):
                    if l2 < hi and lo < h2:
                        d.add(j)
            for ap in rd:
                key, lo, hi = _ap_box(ap)
                readers.setdefault(key, []).append((i, lo, hi))
            for ap in wr:
                key, lo, hi = _ap_box(ap)
                writers.setdefault(key, []).append((i, lo, hi))
            deps[i] = d
            if pending_snapshot is not None:
                barrier_needs[i] = pending_snapshot

        barrier_pos = [i for i, st in enumerate(steps) if st[0] == "BARRIER"]
        barrier_snaps = {}
        for i in barrier_pos:
            snap = {}
            for j in range(i):
                if sem_of[j] is not None:
                    s = sem_of[j]
                    if count_after[j] > snap.get(s, 0):
                        snap[s] = count_after[j]
            barrier_snaps[i] = snap

        sem_ids = sorted(sem_tot.keys())
        import contextlib as _ctx
        with _ctx.ExitStack() as es:
            sems = {s: es.enter_context(nc.semaphore("sem_" + "_".join(str(x) for x in s)))
                    for s in sem_ids}
            with nc.Block() as blk:
                def mk(name):
                    def sec(eng):
                        wm = {s: 0 for s in sem_ids}
                        own = ("c", name)
                        bi = 0
                        pend = {}
                        emitted = False
                        for i, st in enumerate(steps):
                            if st[0] == "BARRIER":
                                for s, v in barrier_snaps[i].items():
                                    if v > pend.get(s, 0):
                                        pend[s] = v
                                continue
                            if st[0] != name:
                                continue
                            need = dict(pend)
                            pend = {}
                            for j in (deps[i] or ()):
                                s = sem_of[j]
                                if s == own and name == "tensor":
                                    continue
                                v = count_after[j]
                                if v > need.get(s, 0):
                                    need[s] = v
                            if self.serial:
                                pass
                            for s in sem_ids:
                                v = need.get(s, 0)
                                if v > wm[s]:
                                    eng.wait_ge(sems[s], v)
                                    wm[s] = v
                            st[2](eng).then_inc(sems[sem_of[i]], st[1])
                            emitted = True
                        if name == "sync":
                            for s in sem_ids:
                                if sem_tot[s] > wm[s]:
                                    eng.wait_ge(sems[s], sem_tot[s])
                        _ = emitted
                    return sec
                for name in ("sync", "tensor", "scalar", "vector", "gpsimd"):
                    getattr(blk, name)(mk(name))


class NCP:

    def __init__(self, nc, pre):
        self._nc = nc
        self._pre = pre

    def sbuf_tensor(self, name, shape, dtype, **kw):
        return self._nc.sbuf_tensor(self._pre + name, shape, dtype, **kw)

    def psum_tensor(self, name, shape, dtype=F32, **kw):
        return self._nc.psum_tensor(self._pre + name, shape, dtype, **kw)

    def __getattr__(self, a):
        return getattr(self._nc, a)


def load_weight_bf16(p, nc, w_hbm, dst_bf, scr_f32, ncols, nchunks):
    half = scr_f32.shape[1] // 2
    i = 0
    for c in range(nchunks):
        for c0 in range(0, ncols, 1024):
            w = min(1024, ncols - c0)
            o = (i % 2) * half if w <= half else 0
            p.dma(scr_f32[:, o:o + w], w_hbm[c * 128:(c + 1) * 128, c0:c0 + w])
            p.vcopy(dst_bf[:, c * ncols + c0: c * ncols + c0 + w],
                    scr_f32[:, o:o + w], eng="gpsimd")
            i += 1


def load_xbf(p, nc, xT_hbm, xbf, scr_f32):
    for c in range(NCHUNK):
        p.dma(scr_f32[:, 0:L], xT_hbm[c * 128:(c + 1) * 128, :])
        p.vcopy(xbf[:, c * L:(c + 1) * L], scr_f32[:, 0:L], eng="gpsimd")


def emit_postln(p, nc, yscr_hbm, ybf, ones_bf, xout_hbm, tiles, fc):
    (ps_r1, ps_c, r_mean, r_var, r_rstd, r_mr, rowb_bf,
     Brstd, Bmr, chunk_scr, chunk_scr2) = tiles
    for ig in range(NIG):
        for c in range(NCHUNK):
            p.mm(ps_r1[:], ones_bf[:, 0:1], ybf[:, c * L + ig * 512: c * L + ig * 512 + 512],
                 start=(c == 0), stop=(c == NCHUNK - 1))
        p.act(r_mean[:, ig * 512:(ig + 1) * 512], ps_r1[:])
    for ig in range(NIG):
        for c in range(NCHUNK):
            sl = ybf[:, c * L + ig * 512: c * L + ig * 512 + 512]
            o = (c % 2) * 512
            p.act(chunk_scr[:, o:o + 512], sl, func=AF.Square)
            p.mm(ps_r1[:], ones_bf[:, 0:1], chunk_scr[:, o:o + 512],
                 start=(c == 0), stop=(c == NCHUNK - 1))
        p.act(r_var[:, ig * 512:(ig + 1) * 512], ps_r1[:])
    p.tsc(r_mean[:], r_mean[:], 1.0 / D, None, ALU.mult)
    p.tsc(r_var[:], r_var[:], 1.0 / D, None, ALU.mult)
    p.tt(r_mr[:], r_mean[:], r_mean[:], ALU.mult)
    p.tt(r_var[:], r_var[:], r_mr[:], ALU.subtract)
    p.act(r_var[:], r_var[:], func=AF.Sqrt, bias=fc[0:1, 3:4])
    p.recip(r_rstd[:], r_var[:])
    p.tt(r_mr[:], r_mean[:], r_rstd[:], ALU.mult)
    for (row, B) in ((r_rstd, Brstd), (r_mr, Bmr)):
        p.act(rowb_bf[:], row[:])
        for ig in range(NIG):
            p.mm(ps_c[:], ones_bf[0:1, 0:128], rowb_bf[:, ig * 512:(ig + 1) * 512])
            p.act(B[:, ig * 512:(ig + 1) * 512], ps_c[:])
    for i in range(2 * NCHUNK):
        c, h = divmod(i, 2)
        o = (i % 2) * 1024
        cs = slice(h * 1024, h * 1024 + 1024)
        p.dma(chunk_scr2[:, o:o + 1024], yscr_hbm[c * 128:(c + 1) * 128, cs])
        p.tt(chunk_scr2[:, o:o + 1024], chunk_scr2[:, o:o + 1024], Brstd[:, cs], ALU.mult)
        p.tt(chunk_scr2[:, o:o + 1024], chunk_scr2[:, o:o + 1024], Bmr[:, cs], ALU.subtract,
             eng="gpsimd")
        p.dma(xout_hbm[c * 128:(c + 1) * 128, cs], chunk_scr2[:, o:o + 1024], eng="gpsimd")


def alloc_ln_tiles(es, nc, chunk_scr2=None):
    ps_r1 = es.enter_context(nc.psum_tensor("ps_ln1", [1, 512], F32))
    ps_c = es.enter_context(nc.psum_tensor("ps_lnc", [128, 512], F32))
    r_mean = es.enter_context(nc.sbuf_tensor("ln_mean", [1, L], F32))
    r_var = es.enter_context(nc.sbuf_tensor("ln_var", [1, L], F32))
    r_rstd = es.enter_context(nc.sbuf_tensor("ln_rstd", [1, L], F32))
    r_mr = es.enter_context(nc.sbuf_tensor("ln_mr", [1, L], F32))
    rowb_bf = es.enter_context(nc.sbuf_tensor("ln_rowbf", [1, L], BF16))
    Brstd = es.enter_context(nc.sbuf_tensor("ln_Brstd", [128, L], F32))
    Bmr = es.enter_context(nc.sbuf_tensor("ln_Bmr", [128, L], F32))
    chunk_scr = es.enter_context(nc.sbuf_tensor("ln_cscr", [128, 1024], BF16))
    if chunk_scr2 is None:
        chunk_scr2 = es.enter_context(nc.sbuf_tensor("ln_cscr2", [128, L], F32))
    return (ps_r1, ps_c, r_mean, r_var, r_rstd, r_mr, rowb_bf,
            Brstd, Bmr, chunk_scr, chunk_scr2)


def emit_attention(p, nc, env, layer):
    (Vt, obf, qxA, qxB, kxA, kxB, sqh, nq_row, km, rcp, rcpb,
     Fsb, tmul, PT, ps_a, ps_o, ps_f, ps_r1) = (
        env["Vt"], env["obf"], env["qxA"], env["qxB"], env["kxA"], env["kxB"],
        env["sqh"], env["nq_row"], env["km"], env["rcp"],
        env["rcpb"], env["Fsb"], env["tmul"], env["PT"], env["ps_a"],
        env["ps_o"], env["ps_f"], env["ps_r1"])
    ones_bf = env["ones_bf"]
    psAB = [ps_a, env["ps_a2"]]
    PTs = [PT, env["PT2"]]
    psOs = [ps_o, env["ps_o2"]]
    tri = env.get("tri")
    causal = layer == 1
    gctr = [0]

    for hp in range(8):
        env["build_qk"](p, hp)
        for (qx, kx) in ((qxA, kxA), (qxB, kxB)):
            for ig in range(NIG):
                p.act(sqh[:, 0:512], kx[0:64, ig * 512:(ig + 1) * 512], func=AF.Square)
                p.mm(ps_r1[:], ones_bf[0:64, 0:1], sqh[:, 0:512])
                p.rmax(km[:, ig:ig + 1], ps_r1[:])
            p.rmax(km[:, 4:5], km[:, 0:4])
            p.act(km[:, 4:5], km[:, 4:5], func=AF.Sqrt)
            for ig in range(NIG):
                p.act(sqh[:, 0:512], qx[0:64, ig * 512:(ig + 1) * 512], func=AF.Square)
                p.mm(ps_r1[:], ones_bf[0:64, 0:1], sqh[:, 0:512])
                p.act(nq_row[:, 0:512], ps_r1[:], func=AF.Sqrt)
                p.tsc(qx[64:65, ig * 512:(ig + 1) * 512], nq_row[:, 0:512],
                      km[0:1, 4:5], -1.0, ALU.mult, ALU.mult)
            p.memset(kx[64:65, :], 1.0)
        for hh in range(2):
            h = 2 * hp + hh
            qx, kx = (qxA, kxA) if hh == 0 else (qxB, kxB)
            for ig in range(NIG):
                njc = min(NIC, 4 * (ig + 1)) if causal else NIC
                po = psOs[gctr[0] % 2]
                gctr[0] += 1
                for jc in range(njc + 1):
                    if jc < njc:
                        pa = psAB[jc % 2]
                        pt = PTs[jc % 2]
                        p.mm(pa[:], kx[:, jc * 128:(jc + 1) * 128],
                             qx[:, ig * 512:(ig + 1) * 512])
                        off = jc * 128 - ig * 512
                        if causal and off >= 0:
                            p.tt(pa[:, off:off + 128], pa[:, off:off + 128], tri[:], ALU.add)
                        p.act(pt[:], pa[:], func=AF.Exp, scale=env["fc"][:, 0:1])
                        if causal and off > 0:
                            p.memset(pt[:, 0:off], 0.0)
                    if jc >= 1:
                        jp = jc - 1
                        p.mm(po[:], Vt[:, jp * 1040 + 65 * h: jp * 1040 + 65 * h + 65],
                             PTs[jp % 2][:], start=(jp == 0), stop=(jp == njc - 1))
                p.recip(rcp[:], po[64:65, :])
                p.act(rcpb[:], rcp[:])
                p.mm(ps_f[:], ones_bf[0:1, 0:64], rcpb[:])
                p.act(Fsb[:], ps_f[:])
                p.tt(tmul[:], po[0:64, :], Fsb[:], ALU.mult)
                bv = env["bv"](hp, hh)
                dst = obf[64 * hh:64 * hh + 64,
                          hp * L + ig * 512: hp * L + ig * 512 + 512]
                if bv is None:
                    p.vcopy(dst, tmul[:], eng="gpsimd")
                else:
                    p.act(dst, tmul[:], bias=bv)


def emit_proj_residual(p, nc, env, w_bf, bT, xT_hbm, yscr_hbm, ybf):
    obf = env["obf"]
    psAB = [env["ps_a"], env["ps_a2"]]
    xchunk = env["xchunk"]
    ychunk = env["ychunk"]
    nxb = xchunk.shape[1] // L
    for oc in range(NCHUNK):
        xo = (oc % nxb) * L
        p.dma(xchunk[:, xo:xo + L], xT_hbm[oc * 128:(oc + 1) * 128, :])
        for ig in range(NIG):
            pa = psAB[ig % 2]
            for fc in range(NCHUNK):
                p.mm(pa[:], w_bf[:, fc * 1024 + oc * 128: fc * 1024 + oc * 128 + 128],
                     obf[:, fc * L + ig * 512: fc * L + ig * 512 + 512],
                     start=(fc == 0), stop=(fc == NCHUNK - 1))
            p.stt(ychunk[:, ig * 512:(ig + 1) * 512], pa[:], bT[:, oc:oc + 1],
                  xchunk[:, xo + ig * 512: xo + ig * 512 + 512], ALU.add, ALU.add)
        p.act(ybf[:, oc * L:(oc + 1) * L], ychunk[:, 0:L])
        p.dma(yscr_hbm[oc * 128:(oc + 1) * 128, :], ychunk[:, 0:L], eng="gpsimd")


L0_ATTN_INPUTS = [("Wqkv", [D, 3 * D], F32), ("bqkvT", [128, 24], F32),
                  ("bvc", [64, 16], F32), ("Wout", [D, D], F32),
                  ("boutT", [128, 8], F32), ("relTc", [128, L], F32)]


def emit_l0_attn(nc, p, t):
    import contextlib
    xT, Wqkv, bqkvT, bvc, Wout, boutT, relTc, onesc, fcc, xout, yscr = (
        t["xT"], t["Wqkv"], t["bqkvT"], t["bvc"], t["Wout"], t["boutT"],
        t["relTc"], t["onesc"], t["fcc"], t["xout"], t["yscr"])
    with contextlib.ExitStack() as es:
        fc = es.enter_context(nc.sbuf_tensor("fc", [128, 8], F32))
        xbf = es.enter_context(nc.sbuf_tensor("xbf", [128, NCHUNK * L], BF16))
        Vt = es.enter_context(nc.sbuf_tensor("Vt", [128, NIC * 1040], BF16))
        obf = es.enter_context(nc.sbuf_tensor("obf", [128, NCHUNK * L], BF16))
        relT = es.enter_context(nc.sbuf_tensor("relT", [128, L], F32))
        ones_bf = es.enter_context(nc.sbuf_tensor("ones_bf", [128, 128], BF16))
        bqkvT_sb = es.enter_context(nc.sbuf_tensor("bqkvT_sb", [128, 24], F32))
        bv_sb = es.enter_context(nc.sbuf_tensor("bv_sb", [64, 16], F32))
        boutT_sb = es.enter_context(nc.sbuf_tensor("boutT_sb", [128, 8], F32))
        scr_f32 = es.enter_context(nc.sbuf_tensor("scr_f32", [128, 3 * D], F32))
        ps_a = es.enter_context(nc.psum_tensor("ps_a", [128, 512], F32))
        ps_a2 = es.enter_context(nc.psum_tensor("ps_a2", [128, 512], F32))
        ps_o = es.enter_context(nc.psum_tensor("ps_o", [65, 512], F32))
        ps_o2 = es.enter_context(nc.psum_tensor("ps_o2", [65, 512], F32))
        ps_f = es.enter_context(nc.psum_tensor("ps_f", [64, 512], F32))
        ps_r1 = es.enter_context(nc.psum_tensor("ps_r1", [1, 512], F32))
        psAB = [ps_a, ps_a2]

        p.dma(fc[:], fcc)
        p.dma(ones_bf[:], onesc)
        p.dma(relT[:], relTc)
        p.dma(bqkvT_sb[:], bqkvT)
        p.dma(bv_sb[:], bvc)
        p.dma(boutT_sb[:], boutT)
        load_xbf(p, nc, xT, xbf, scr_f32)

        with contextlib.ExitStack() as es2:
            wqkv_bf = es2.enter_context(nc.sbuf_tensor("wqkv_bf", [128, NCHUNK * 3 * D], BF16))
            qxA = es2.enter_context(nc.sbuf_tensor("qxA", [65, L], BF16))
            qxB = es2.enter_context(nc.sbuf_tensor("qxB", [65, L], BF16))
            kxA = es2.enter_context(nc.sbuf_tensor("kxA", [65, L], BF16))
            kxB = es2.enter_context(nc.sbuf_tensor("kxB", [65, L], BF16))
            kf32 = es2.enter_context(nc.sbuf_tensor("kf32", [128, 512], F32))
            sqh = es2.enter_context(nc.sbuf_tensor("sqh", [64, 512], BF16))
            nq_row = es2.enter_context(nc.sbuf_tensor("nq_row", [1, 512], F32))
            km = es2.enter_context(nc.sbuf_tensor("km", [1, 5], F32))
            rcp = es2.enter_context(nc.sbuf_tensor("rcp", [1, 512], F32))
            rcpb = es2.enter_context(nc.sbuf_tensor("rcpb", [1, 512], BF16))
            Fsb = es2.enter_context(nc.sbuf_tensor("Fsb", [64, 512], F32))
            tmul = es2.enter_context(nc.sbuf_tensor("tmul", [64, 512], F32))
            PT = es2.enter_context(nc.sbuf_tensor("PT", [128, 512], BF16))
            PT2 = es2.enter_context(nc.sbuf_tensor("PT2", [128, 512], BF16))

            load_weight_bf16(p, nc, Wqkv, wqkv_bf, scr_f32, 3 * D, NCHUNK)

            p.memset(Vt.rearrange("p (j c) -> p j c", c=65)[:, :, 64:65], 1.0)
            for ic in range(NIC):
                for hg in range(2):
                    pa = psAB[(2 * ic + hg) % 2]
                    for dc in range(NCHUNK):
                        p.mm(pa[:], xbf[:, dc * L + ic * 128: dc * L + ic * 128 + 128],
                             wqkv_bf[:, dc * 3 * D + 2 * D + hg * 512: dc * 3 * D + 2 * D + hg * 512 + 512],
                             start=(dc == 0), stop=(dc == NCHUNK - 1))
                    dst = Vt[:, ic * 1040 + hg * 520: ic * 1040 + hg * 520 + 520]
                    dst3 = dst.rearrange("p (h c) -> p h c", c=65)[:, :, 0:64]
                    src3 = pa.rearrange("p (h c) -> p h c", c=64)
                    p.vcopy(dst3, src3)

            def build_qk(p, hp):
                for ig in range(NIG):
                    pa = psAB[ig % 2]
                    for dc in range(NCHUNK):
                        p.mm(pa[:], wqkv_bf[:, dc * 3 * D + hp * 128: dc * 3 * D + hp * 128 + 128],
                             xbf[:, dc * L + ig * 512: dc * L + ig * 512 + 512],
                             start=(dc == 0), stop=(dc == NCHUNK - 1))
                    p.act(qxA[0:64, ig * 512:(ig + 1) * 512], pa[0:64, :],
                          bias=bqkvT_sb[0:64, hp:hp + 1])
                    p.act(qxB[0:64, ig * 512:(ig + 1) * 512], pa[64:128, :],
                          bias=bqkvT_sb[64:128, hp:hp + 1])
                for ig in range(NIG):
                    pa = psAB[ig % 2]
                    for dc in range(NCHUNK):
                        p.mm(pa[:], wqkv_bf[:, dc * 3 * D + D + hp * 128: dc * 3 * D + D + hp * 128 + 128],
                             xbf[:, dc * L + ig * 512: dc * L + ig * 512 + 512],
                             start=(dc == 0), stop=(dc == NCHUNK - 1))
                    p.act(kf32[:, 0:512], pa[:], bias=bqkvT_sb[:, 8 + hp: 9 + hp])
                    p.tt(kxA[0:64, ig * 512:(ig + 1) * 512], kf32[0:64, 0:512],
                         relT[0:64, ig * 512:(ig + 1) * 512], ALU.add)
                    p.tt(kxB[0:64, ig * 512:(ig + 1) * 512], kf32[64:128, 0:512],
                         relT[64:128, ig * 512:(ig + 1) * 512], ALU.add,
                         eng="gpsimd")

            def bv(hp, hh):
                return bv_sb[:, 2 * hp + hh: 2 * hp + hh + 1]

            env = dict(Vt=Vt, obf=obf, qxA=qxA, qxB=qxB, kxA=kxA, kxB=kxB,
                       sqh=sqh, nq_row=nq_row, km=km, rcp=rcp,
                       rcpb=rcpb, Fsb=Fsb, tmul=tmul, PT=PT, PT2=PT2,
                       ps_a=ps_a, ps_a2=ps_a2,
                       ps_o=ps_o, ps_o2=ps_o2, ps_f=ps_f, ps_r1=ps_r1, ones_bf=ones_bf,
                       build_qk=build_qk, bv=bv, fc=fc)
            emit_attention(p, nc, env, layer=0)
        p.barrier()

        with contextlib.ExitStack() as es3:
            wout_bf = es3.enter_context(nc.sbuf_tensor("wout_bf", [128, NCHUNK * D], BF16))
            xchunk = es3.enter_context(nc.sbuf_tensor("xchunk", [128, L], F32))
            ychunk = es3.enter_context(nc.sbuf_tensor("ychunk", [128, L], F32))
            ln = alloc_ln_tiles(es3, nc, chunk_scr2=ychunk)
            load_weight_bf16(p, nc, Wout, wout_bf, scr_f32, D, NCHUNK)
            env2 = dict(obf=obf, ps_a=ps_a, ps_a2=ps_a2, xchunk=xchunk, ychunk=ychunk)
            ybf = xbf
            emit_proj_residual(p, nc, env2, wout_bf, boutT_sb, xT, yscr, ybf)
            emit_postln(p, nc, yscr, ybf, ones_bf, xout, ln, fc)


L0_FFN_INPUTS = [("WT", [D, D], F32), ("bT", [128, 8], F32),
                 ("ldT", [100, 4], F32), ("luT", [4, 50], F32),
                 ("dn", [100, D], F32), ("uaT", [50, D], F32)]


def emit_l0_ffn(nc, p, t):
    import contextlib
    xT, WT, bT, ldT, luT, dn, uaT, onesc, fcc, xout, yscr = (
        t["xT"], t["WT"], t["bT"], t["ldT"], t["luT"], t["dn"], t["uaT"],
        t["onesc"], t["fcc"], t["xout"], t["yscr"])
    with contextlib.ExitStack() as es:
        fc = es.enter_context(nc.sbuf_tensor("fc", [128, 8], F32))
        xbf = es.enter_context(nc.sbuf_tensor("xbf", [128, NCHUNK * L], BF16))
        ones_bf = es.enter_context(nc.sbuf_tensor("ones_bf", [128, 128], BF16))
        bT_sb = es.enter_context(nc.sbuf_tensor("bT_sb", [128, 8], F32))
        scr_f32 = es.enter_context(nc.sbuf_tensor("scr_f32", [128, 2048], F32))
        weff = es.enter_context(nc.sbuf_tensor("weff", [128, NCHUNK * D], BF16))
        ldT_sb = es.enter_context(nc.sbuf_tensor("ldT_sb", [100, 4], F32))
        luT_sb = es.enter_context(nc.sbuf_tensor("luT_sb", [4, 50], F32))
        dn_sb = es.enter_context(nc.sbuf_tensor("dn_sb", [100, D], F32))
        uaT_sb = es.enter_context(nc.sbuf_tensor("uaT_sb", [50, D], F32))
        D2_sb = es.enter_context(nc.sbuf_tensor("D2_sb", [4, D], F32))
        B_sb = es.enter_context(nc.sbuf_tensor("B_sb", [50, D], F32))
        xchunk = es.enter_context(nc.sbuf_tensor("xchunk", [128, 2 * L], F32))
        ychunk = es.enter_context(nc.sbuf_tensor("ychunk", [128, L], F32))
        ps_a = es.enter_context(nc.psum_tensor("ps_a", [128, 512], F32))
        ps_a2 = es.enter_context(nc.psum_tensor("ps_a2", [128, 512], F32))
        psAB = [ps_a, ps_a2]
        ps_b = es.enter_context(nc.psum_tensor("ps_b", [50, 512], F32))
        ln = alloc_ln_tiles(es, nc)

        p.dma(fc[:], fcc)
        p.dma(ones_bf[:], onesc)
        p.dma(bT_sb[:], bT)
        p.dma(ldT_sb[:], ldT)
        p.dma(luT_sb[:], luT)
        p.dma(dn_sb[:], dn)
        p.dma(uaT_sb[:], uaT)
        load_xbf(p, nc, xT, xbf, scr_f32)

        for g in range(2):
            p.mm(ps_b[0:4, :], ldT_sb[:], dn_sb[:, g * 512:(g + 1) * 512])
            p.vcopy(D2_sb[:, g * 512:(g + 1) * 512], ps_b[0:4, :])
        for g in range(2):
            p.mm(ps_b[:], luT_sb[:], D2_sb[:, g * 512:(g + 1) * 512])
            p.vcopy(B_sb[:, g * 512:(g + 1) * 512], ps_b[:])
        for c in range(NCHUNK):
            so = (c % 2) * D
            p.dma(scr_f32[:, so:so + D], WT[c * 128:(c + 1) * 128, :])
            for g in range(2):
                pa = psAB[g]
                p.mm(pa[:], B_sb[:, c * 128:(c + 1) * 128], uaT_sb[:, g * 512:(g + 1) * 512])
                p.tt(weff[:, c * D + g * 512: c * D + g * 512 + 512], pa[:],
                     scr_f32[:, so + g * 512: so + g * 512 + 512], ALU.add)
        ybf = xbf
        for oc in range(NCHUNK):
            xo = (oc % 2) * L
            p.dma(xchunk[:, xo:xo + L], xT[oc * 128:(oc + 1) * 128, :])
            for ig in range(NIG):
                pa = psAB[ig % 2]
                for ic in range(NCHUNK):
                    p.mm(pa[:], weff[:, ic * D + oc * 128: ic * D + oc * 128 + 128],
                         xbf[:, ic * L + ig * 512: ic * L + ig * 512 + 512],
                         start=(ic == 0), stop=(ic == NCHUNK - 1))
                p.stt(ychunk[:, ig * 512:(ig + 1) * 512], pa[:], bT_sb[:, oc:oc + 1],
                      xchunk[:, xo + ig * 512: xo + ig * 512 + 512], ALU.add, ALU.add)
            p.dma(yscr[oc * 128:(oc + 1) * 128, :], ychunk[:, 0:L], eng="gpsimd")
        load_xbf(p, nc, yscr, ybf, ychunk)
        emit_postln(p, nc, yscr, ybf, ones_bf, xout, ln, fc)


L1_ATTN_INPUTS = [("WqT", [D, D], F32), ("WkT", [D, D], F32),
                  ("WvT", [D, D], F32), ("WoT", [D, D], F32),
                  ("boT", [128, 8], F32), ("qg", [64, 1], F32),
                  ("qb", [64, 1], F32), ("kg", [64, 1], F32),
                  ("kb", [64, 1], F32), ("cosc", [64, L], BF16),
                  ("sinc", [64, L], BF16), ("rotc", [64, 64], BF16),
                  ("tric", [128, 128], F32)]


def emit_l1_attn(nc, p, t):
    import contextlib
    (xT, WqT, WkT, WvT, WoT, boT, qg, qb, kg, kb, cosc, sinc, rotc, tric,
     onesc, fcc, xout, yscr) = (
        t["xT"], t["WqT"], t["WkT"], t["WvT"], t["WoT"], t["boT"], t["qg"],
        t["qb"], t["kg"], t["kb"], t["cosc"], t["sinc"], t["rotc"], t["tric"],
        t["onesc"], t["fcc"], t["xout"], t["yscr"])
    with contextlib.ExitStack() as es:
        fc = es.enter_context(nc.sbuf_tensor("fc", [128, 8], F32))
        xbf = es.enter_context(nc.sbuf_tensor("xbf", [128, NCHUNK * L], BF16))
        Vt = es.enter_context(nc.sbuf_tensor("Vt", [128, NIC * 1040], BF16))
        obf = es.enter_context(nc.sbuf_tensor("obf", [128, NCHUNK * L], BF16))
        ones_bf = es.enter_context(nc.sbuf_tensor("ones_bf", [128, 128], BF16))
        cosT = es.enter_context(nc.sbuf_tensor("cosT", [64, L], BF16))
        sinT = es.enter_context(nc.sbuf_tensor("sinT", [64, L], BF16))
        rotT = es.enter_context(nc.sbuf_tensor("rotT", [64, 64], BF16))
        tri = es.enter_context(nc.sbuf_tensor("tri", [128, 128], F32))
        gb_sb = es.enter_context(nc.sbuf_tensor("gb_sb", [64, 4], F32))
        boT_sb = es.enter_context(nc.sbuf_tensor("boT_sb", [128, 8], F32))
        scr_f32 = es.enter_context(nc.sbuf_tensor("scr_f32", [128, 2048], F32))
        ps_a = es.enter_context(nc.psum_tensor("ps_a", [128, 512], F32))
        ps_a2 = es.enter_context(nc.psum_tensor("ps_a2", [128, 512], F32))
        psAB = [ps_a, ps_a2]
        ps_o = es.enter_context(nc.psum_tensor("ps_o", [65, 512], F32))
        ps_o2 = es.enter_context(nc.psum_tensor("ps_o2", [65, 512], F32))
        ps_f = es.enter_context(nc.psum_tensor("ps_f", [64, 512], F32))
        ps_r1 = es.enter_context(nc.psum_tensor("ps_r1", [1, 512], F32))

        p.dma(fc[:], fcc)
        p.dma(ones_bf[:], onesc)
        p.dma(cosT[:], cosc)
        p.dma(sinT[:], sinc)
        p.dma(rotT[:], rotc)
        p.dma(tri[:], tric)
        p.dma(gb_sb[:, 0:1], qg)
        p.dma(gb_sb[:, 1:2], qb)
        p.dma(gb_sb[:, 2:3], kg)
        p.dma(gb_sb[:, 3:4], kb)
        p.dma(boT_sb[:], boT)
        load_xbf(p, nc, xT, xbf, scr_f32)

        with contextlib.ExitStack() as esv:
            wvT_bf = esv.enter_context(nc.sbuf_tensor("wvT_bf", [128, NCHUNK * D], BF16))
            load_weight_bf16(p, nc, WvT, wvT_bf, scr_f32, D, NCHUNK)
            p.memset(Vt.rearrange("p (j c) -> p j c", c=65)[:, :, 64:65], 1.0)
            for ic in range(NIC):
                for hg in range(2):
                    pa = psAB[(2 * ic + hg) % 2]
                    for dc in range(NCHUNK):
                        p.mm(pa[:], xbf[:, dc * L + ic * 128: dc * L + ic * 128 + 128],
                             wvT_bf[:, dc * D + hg * 512: dc * D + hg * 512 + 512],
                             start=(dc == 0), stop=(dc == NCHUNK - 1))
                    dst = Vt[:, ic * 1040 + hg * 520: ic * 1040 + hg * 520 + 520]
                    dst3 = dst.rearrange("p (h c) -> p h c", c=65)[:, :, 0:64]
                    src3 = pa.rearrange("p (h c) -> p h c", c=64)
                    p.vcopy(dst3, src3)
        p.barrier()

        with contextlib.ExitStack() as es2:
            wqT_bf = es2.enter_context(nc.sbuf_tensor("wqT_bf", [128, NCHUNK * D], BF16))
            wkT_bf = es2.enter_context(nc.sbuf_tensor("wkT_bf", [128, NCHUNK * D], BF16))
            qxA = es2.enter_context(nc.sbuf_tensor("qxA", [65, L], BF16))
            qxB = es2.enter_context(nc.sbuf_tensor("qxB", [65, L], BF16))
            kxA = es2.enter_context(nc.sbuf_tensor("kxA", [65, L], BF16))
            kxB = es2.enter_context(nc.sbuf_tensor("kxB", [65, L], BF16))
            qfA = es2.enter_context(nc.sbuf_tensor("qfA", [64, 512], F32))
            qfB = es2.enter_context(nc.sbuf_tensor("qfB", [64, 512], F32))
            hbf = es2.enter_context(nc.sbuf_tensor("hbf", [64, 512], BF16))
            sqh = es2.enter_context(nc.sbuf_tensor("sqh", [64, 512], BF16))
            st_s = es2.enter_context(nc.sbuf_tensor("st_s", [1, 512], F32))
            st_q = es2.enter_context(nc.sbuf_tensor("st_q", [1, 512], F32))
            st_a = es2.enter_context(nc.sbuf_tensor("st_a", [1, 512], F32))
            rs_bf = es2.enter_context(nc.sbuf_tensor("rs_bf", [1, 512], BF16))
            mr_bf = es2.enter_context(nc.sbuf_tensor("mr_bf", [1, 512], BF16))
            Brstd = es2.enter_context(nc.sbuf_tensor("Brstd_h", [64, 512], F32))
            Bmr = es2.enter_context(nc.sbuf_tensor("Bmr_h", [64, 512], F32))
            t1 = es2.enter_context(nc.sbuf_tensor("t1", [64, 512], F32))
            t2 = es2.enter_context(nc.sbuf_tensor("t2", [64, 512], F32))
            qln = es2.enter_context(nc.sbuf_tensor("qln", [64, 512], BF16))
            nq_row = es2.enter_context(nc.sbuf_tensor("nq_row", [1, 512], F32))
            km = es2.enter_context(nc.sbuf_tensor("km", [1, 5], F32))
            rcp = es2.enter_context(nc.sbuf_tensor("rcp", [1, 512], F32))
            rcpb = es2.enter_context(nc.sbuf_tensor("rcpb", [1, 512], BF16))
            Fsb = es2.enter_context(nc.sbuf_tensor("Fsb", [64, 512], F32))
            tmul = es2.enter_context(nc.sbuf_tensor("tmul", [64, 512], F32))
            PT = es2.enter_context(nc.sbuf_tensor("PT", [128, 512], BF16))
            PT2 = es2.enter_context(nc.sbuf_tensor("PT2", [128, 512], BF16))

            load_weight_bf16(p, nc, WqT, wqT_bf, scr_f32, D, NCHUNK)
            load_weight_bf16(p, nc, WkT, wkT_bf, scr_f32, D, NCHUNK)

            def lnhead(p, hp, w_bf, gcol, bcol, xdstA, xdstB):
                for ig in range(NIG):
                    sl = slice(ig * 512, (ig + 1) * 512)
                    pa = psAB[ig % 2]
                    for dc in range(NCHUNK):
                        p.mm(pa[:], w_bf[:, dc * D + hp * 128: dc * D + hp * 128 + 128],
                             xbf[:, dc * L + ig * 512: dc * L + ig * 512 + 512],
                             start=(dc == 0), stop=(dc == NCHUNK - 1))
                    p.act(qfA[:], pa[0:64, :])
                    p.act(qfB[:], pa[64:128, :])
                    for hh, xdst in ((0, xdstA), (1, xdstB)):
                        rows = qfA[:] if hh == 0 else qfB[:]
                        p.vcopy(hbf[:], rows, eng="gpsimd")
                        p.mm(ps_r1[:], ones_bf[0:64, 0:1], hbf[:])
                        p.act(st_s[:], ps_r1[:], scale=1.0 / DH)
                        p.act(sqh[:, 0:512], hbf[:], func=AF.Square)
                        p.mm(ps_r1[:], ones_bf[0:64, 0:1], sqh[:, 0:512])
                        p.act(st_q[:], ps_r1[:], scale=1.0 / DH)
                        p.tt(st_a[:], st_s[:], st_s[:], ALU.mult)
                        p.tt(st_q[:], st_q[:], st_a[:], ALU.subtract)
                        p.act(st_q[:], st_q[:], func=AF.Sqrt, bias=fc[0:1, 3:4])
                        p.recip(st_a[:], st_q[:])
                        p.tt(st_q[:], st_s[:], st_a[:], ALU.mult)
                        p.act(rs_bf[:], st_a[:])
                        p.act(mr_bf[:], st_q[:])
                        p.mm(ps_f[:], ones_bf[0:1, 0:64], rs_bf[:])
                        p.act(Brstd[:], ps_f[:])
                        p.mm(ps_f[:], ones_bf[0:1, 0:64], mr_bf[:])
                        p.act(Bmr[:], ps_f[:])
                        p.tt(t1[:], rows, Brstd[:], ALU.mult)
                        p.tt(t1[:], t1[:], Bmr[:], ALU.subtract)
                        p.act(qln[:], t1[:], bias=gb_sb[:, bcol:bcol + 1],
                              scale=gb_sb[:, gcol:gcol + 1])
                        p.mm(ps_f[:], rotT[:], qln[:])
                        p.tt(t1[:], qln[:], cosT[:, sl], ALU.mult)
                        p.tt(t2[:], ps_f[:], sinT[:, sl], ALU.mult)
                        p.tt(xdst[0:64, sl], t1[:], t2[:], ALU.add)

            def build_qk(p, hp):
                lnhead(p, hp, wqT_bf, 0, 1, qxA, qxB)
                lnhead(p, hp, wkT_bf, 2, 3, kxA, kxB)

            def bv(hp, hh):
                return None

            env = dict(Vt=Vt, obf=obf, qxA=qxA, qxB=qxB, kxA=kxA, kxB=kxB,
                       sqh=sqh, nq_row=nq_row, km=km, rcp=rcp,
                       rcpb=rcpb, Fsb=Fsb, tmul=tmul, PT=PT, PT2=PT2,
                       ps_a=ps_a, ps_a2=ps_a2,
                       ps_o=ps_o, ps_o2=ps_o2, ps_f=ps_f, ps_r1=ps_r1, ones_bf=ones_bf,
                       build_qk=build_qk, bv=bv, tri=tri, fc=fc)
            emit_attention(p, nc, env, layer=1)
        p.barrier()

        with contextlib.ExitStack() as es3:
            woT_bf = es3.enter_context(nc.sbuf_tensor("woT_bf", [128, NCHUNK * D], BF16))
            xchunk = es3.enter_context(nc.sbuf_tensor("xchunk", [128, 2 * L], F32))
            ychunk = es3.enter_context(nc.sbuf_tensor("ychunk", [128, L], F32))
            ln = alloc_ln_tiles(es3, nc, chunk_scr2=ychunk)
            load_weight_bf16(p, nc, WoT, woT_bf, scr_f32, D, NCHUNK)
            env2 = dict(obf=obf, ps_a=ps_a, ps_a2=ps_a2, xchunk=xchunk, ychunk=ychunk)
            ybf = xbf
            emit_proj_residual(p, nc, env2, woT_bf, boT_sb, xT, yscr, ybf)
            emit_postln(p, nc, yscr, ybf, ones_bf, xout, ln, fc)


L1_FFN_INPUTS = [("k0t", [D, 4096], U8), ("k0b", [D, 4096], U8),
                 ("k1", [4096, 4096], U8), ("k2", [4096, D], U8),
                 ("s0", [1, 1], F32), ("s1", [1, 1], F32), ("s2", [1, 1], F32)]


def emit_l1_ffn(nc, p, t):
    import contextlib
    (xT, k0t, k0b, k1, k2, s0, s1, s2, onesc, onesf, fcc, xout, yscr,
     a2scr) = (
        t["xT"], t["k0t"], t["k0b"], t["k1"], t["k2"], t["s0"], t["s1"],
        t["s2"], t["onesc"], t["onesf"], t["fcc"], t["xout"], t["yscr"],
        t["a2scr"])
    with contextlib.ExitStack() as es:
        fc = es.enter_context(nc.sbuf_tensor("fc", [128, 8], F32))
        xbf = es.enter_context(nc.sbuf_tensor("xbf", [128, NCHUNK * L], BF16))
        ones_bf = es.enter_context(nc.sbuf_tensor("ones_bf", [128, 128], BF16))
        ones_f = es.enter_context(nc.sbuf_tensor("ones_f", [1, 128], F32))
        s_sb = es.enter_context(nc.sbuf_tensor("s_sb", [1, 3], F32))
        sb_b = es.enter_context(nc.sbuf_tensor("sb_b", [128, 3], F32))
        scr_f32 = es.enter_context(nc.sbuf_tensor("scr_f32", [128, 2048], F32))
        ps_a = es.enter_context(nc.psum_tensor("ps_a", [128, 512], F32))
        ps_a2 = es.enter_context(nc.psum_tensor("ps_a2", [128, 512], F32))
        psAB = [ps_a, ps_a2]
        ps_c = es.enter_context(nc.psum_tensor("ps_cb", [128, 1], F32))

        p.dma(fc[:], fcc)
        p.dma(ones_bf[:], onesc)
        p.dma(ones_f[:], onesf)
        p.dma(s_sb[:, 0:1], s0)
        p.dma(s_sb[:, 1:2], s1)
        p.dma(s_sb[:, 2:3], s2)
        for j, sc in ((0, 7), (1, 1), (2, None)):
            p.mm(ps_c[:], ones_f[:], s_sb[:, j:j + 1])
            if sc is not None:
                p.act(sb_b[:, j:j + 1], ps_c[:], scale=fc[:, sc:sc + 1])
            else:
                p.act(sb_b[:, j:j + 1], ps_c[:])
        load_xbf(p, nc, xT, xbf, scr_f32)

        with contextlib.ExitStack() as esA:
            a1 = esA.enter_context(nc.sbuf_tensor("a1", [128, 32 * L], BF16))
            ka = esA.enter_context(nc.sbuf_tensor("ka", [128, 256], U8))
            kb_ = esA.enter_context(nc.sbuf_tensor("kb_", [128, 256], U8))
            ta = esA.enter_context(nc.sbuf_tensor("ta", [128, 256], BF16))
            tb = esA.enter_context(nc.sbuf_tensor("tb", [128, 256], BF16))
            wf = esA.enter_context(nc.sbuf_tensor("wf", [128, 2 * NCHUNK * 128], BF16))
            for f1c in range(32):
                wo = (f1c % 2) * NCHUNK * 128
                for ic in range(NCHUNK):
                    o = (ic % 2) * 128
                    p.dma(ka[:, o:o + 128], k0t[ic * 128:(ic + 1) * 128, f1c * 128:(f1c + 1) * 128])
                    p.dma(kb_[:, o:o + 128], k0b[ic * 128:(ic + 1) * 128, f1c * 128:(f1c + 1) * 128])
                    p.vcopy(ta[:, o:o + 128], ka[:, o:o + 128], eng="gpsimd")
                    p.vcopy(tb[:, o:o + 128], kb_[:, o:o + 128], eng="gpsimd")
                    p.tt(wf[:, wo + ic * 128: wo + ic * 128 + 128],
                         ta[:, o:o + 128], tb[:, o:o + 128], ALU.subtract)
                for ig in range(NIG):
                    pa = psAB[ig % 2]
                    for ic in range(NCHUNK):
                        p.mm(pa[:], wf[:, wo + ic * 128: wo + ic * 128 + 128],
                             xbf[:, ic * L + ig * 512: ic * L + ig * 512 + 512],
                             start=(ic == 0), stop=(ic == NCHUNK - 1))
                    p.act(a1[:, f1c * L + ig * 512: f1c * L + ig * 512 + 512], pa[:],
                          func=AF.Relu, scale=sb_b[:, 0:1])

            with contextlib.ExitStack() as esB:
                k1s = esB.enter_context(nc.sbuf_tensor("k1s", [128, 4096], U8))
                w1 = esB.enter_context(nc.sbuf_tensor("w1", [128, 4096], BF16))
                a2c = esB.enter_context(nc.sbuf_tensor("a2c", [128, L], BF16))
                for f2c in range(32):
                    for q in range(4):
                        src = k1[q * 1024:(q + 1) * 1024, f2c * 128:(f2c + 1) * 128]
                        src3 = src.rearrange("(c p) f -> p c f", p=128)
                        dst3 = k1s[:, q * 1024:(q + 1) * 1024].rearrange("p (c f) -> p c f", f=128)
                        p.dma(dst3, src3)
                    for hlf in range(2):
                        hs = slice(hlf * 2048, hlf * 2048 + 2048)
                        p.tsc(w1[:, hs], k1s[:, hs], 2.0, -1.0, ALU.mult, ALU.add,
                              eng="gpsimd")
                    for ig in range(NIG):
                        pa = psAB[ig % 2]
                        for ic in range(32):
                            p.mm(pa[:], w1[:, ic * 128:(ic + 1) * 128],
                                 a1[:, ic * L + ig * 512: ic * L + ig * 512 + 512],
                                 start=(ic == 0), stop=(ic == 31))
                        p.act(a2c[:, ig * 512:(ig + 1) * 512], pa[:],
                              func=AF.Relu, scale=sb_b[:, 1:2])
                    p.dma(a2scr[f2c * 128:(f2c + 1) * 128, :], a2c[:], eng="gpsimd")
        p.barrier()

        with contextlib.ExitStack() as esC:
            w2 = esC.enter_context(nc.sbuf_tensor("w2", [128, 32 * D], BF16))
            k2s = esC.enter_context(nc.sbuf_tensor("k2s", [128, D], U8))
            a2g = esC.enter_context(nc.sbuf_tensor("a2g", [128, 32 * 512], BF16))
            xchunk = esC.enter_context(nc.sbuf_tensor("xchunk", [128, 1024], F32))
            tch = esC.enter_context(nc.sbuf_tensor("tch", [128, 1024], F32))
            ybf = xbf
            for fc2 in range(32):
                p.dma(k2s[:], k2[fc2 * 128:(fc2 + 1) * 128, :])
                p.tsc(w2[:, fc2 * D:(fc2 + 1) * D], k2s[:],
                      2.0, -1.0, ALU.mult, ALU.add, eng="gpsimd")
            for ig in range(NIG):
                for c2 in range(32):
                    p.dma(a2g[:, c2 * 512:(c2 + 1) * 512],
                          a2scr[c2 * 128:(c2 + 1) * 128, ig * 512:(ig + 1) * 512])
                for oc in range(NCHUNK):
                    pa = psAB[oc % 2]
                    o = (oc % 2) * 512
                    for c2 in range(32):
                        p.mm(pa[:], w2[:, c2 * D + oc * 128: c2 * D + oc * 128 + 128],
                             a2g[:, c2 * 512:(c2 + 1) * 512],
                             start=(c2 == 0), stop=(c2 == 31))
                    p.dma(xchunk[:, o:o + 512], xT[oc * 128:(oc + 1) * 128, ig * 512:(ig + 1) * 512])
                    p.act(tch[:, o:o + 512], pa[:], scale=sb_b[:, 2:3])
                    p.tt(tch[:, o:o + 512], tch[:, o:o + 512], xchunk[:, o:o + 512], ALU.add)
                    p.vcopy(ybf[:, oc * L + ig * 512: oc * L + ig * 512 + 512],
                            tch[:, o:o + 512], eng="gpsimd")
                    p.dma(yscr[oc * 128:(oc + 1) * 128, ig * 512:(ig + 1) * 512],
                          tch[:, o:o + 512], eng="gpsimd")
            with contextlib.ExitStack() as esL:
                ln = alloc_ln_tiles(esL, nc)
                emit_postln(p, nc, yscr, ybf, ones_bf, xout, ln, fc)


PHASES = [("p1_", emit_l0_attn, L0_ATTN_INPUTS),
          ("p2_", emit_l0_ffn, L0_FFN_INPUTS),
          ("p3_", emit_l1_attn, L1_ATTN_INPUTS),
          ("p4_", emit_l1_ffn, L1_FFN_INPUTS)]


def build_fused():
    nc = bass.Bass("TRN2", target_bir_lowering=False, debug=False)
    onesc = nc.dram_tensor("onesc", [128, 128], BF16, kind="ExternalInput").ap()
    onesf = nc.dram_tensor("onesf", [1, 128], F32, kind="ExternalInput").ap()
    fcc = nc.dram_tensor("fcc", [128, 8], F32, kind="ExternalInput").ap()
    xin = nc.dram_tensor("xT", [D, L], F32, kind="ExternalInput").ap()
    xout = nc.dram_tensor("xout", [D, L], F32, kind="ExternalOutput").ap()
    yscr = nc.dram_tensor("yscr", [D, L], F32, kind="Internal").ap()
    a2scr = nc.dram_tensor("a2scr", [4096, L], BF16, kind="Internal").ap()
    xmid = [nc.dram_tensor(f"xmid{i}", [D, L], F32, kind="Internal").ap()
            for i in range(3)]
    p = P(nc)
    chain = [xin, xmid[0], xmid[1], xmid[2], xout]
    for i, (pre, emit, specs) in enumerate(PHASES):
        t = {"onesc": onesc, "onesf": onesf, "fcc": fcc, "yscr": yscr,
             "a2scr": a2scr, "xT": chain[i], "xout": chain[i + 1]}
        for (nm, shp, dt) in specs:
            t[nm] = nc.dram_tensor(pre + nm, shp, dt, kind="ExternalInput").ap()
        emit(NCP(nc, pre), p, t)
        p.barrier()
    p.build()
    return nc


def _phase_arrays(inp):
    f32 = np.float32
    ones_bf, ones_f, cosT, sinT, rotT, tri, fcc = _consts()
    relT = np.zeros((128, L), dtype=f32)
    relT[0:64, 1:] = np.asarray(inp["L0_attn_rel_emb"])[0:L - 1].T
    relT[64:128] = relT[0:64]
    k0 = np.asarray(inp["L1_ffn_k0"]).view(np.uint8)
    com = {
        "onesc": ones_bf, "onesf": ones_f, "fcc": fcc,
        "Wqkv": np.ascontiguousarray(inp["L0_attn_Wqkv"], dtype=f32),
        "bqkvT": np.ascontiguousarray(np.asarray(inp["L0_attn_bqkv"], dtype=f32).reshape(24, 128).T),
        "bvc": np.ascontiguousarray(np.asarray(inp["L0_attn_bqkv"], dtype=f32)[2048:].reshape(16, 64).T),
        "Wout": np.ascontiguousarray(inp["L0_attn_Wout"], dtype=f32),
        "boutT": np.ascontiguousarray(np.asarray(inp["L0_attn_bout"], dtype=f32).reshape(8, 128).T),
        "relTc": relT,
        "WT": np.ascontiguousarray(np.asarray(inp["L0_ffn_W"], dtype=f32).T),
        "bT": np.ascontiguousarray(np.asarray(inp["L0_ffn_b"], dtype=f32).reshape(8, 128).T),
        "ldT": np.ascontiguousarray(np.asarray(inp["L0_ffn_lora_down"], dtype=f32).T),
        "luT": np.ascontiguousarray(np.asarray(inp["L0_ffn_lora_up"], dtype=f32).T),
        "dn": np.ascontiguousarray(inp["L0_ffn_down_aux"], dtype=f32),
        "uaT": np.ascontiguousarray(np.asarray(inp["L0_ffn_up_aux"], dtype=f32).T),
        "WqT": np.ascontiguousarray(np.asarray(inp["L1_attn_Wq"], dtype=f32).T),
        "WkT": np.ascontiguousarray(np.asarray(inp["L1_attn_Wk"], dtype=f32).T),
        "WvT": np.ascontiguousarray(np.asarray(inp["L1_attn_Wv"], dtype=f32).T),
        "WoT": np.ascontiguousarray(np.asarray(inp["L1_attn_Wo"], dtype=f32).T),
        "boT": np.ascontiguousarray(np.asarray(inp["L1_attn_bo"], dtype=f32).reshape(8, 128).T),
        "qg": np.asarray(inp["L1_attn_q_gamma"], dtype=f32).reshape(64, 1),
        "qb": np.asarray(inp["L1_attn_q_beta"], dtype=f32).reshape(64, 1),
        "kg": np.asarray(inp["L1_attn_k_gamma"], dtype=f32).reshape(64, 1),
        "kb": np.asarray(inp["L1_attn_k_beta"], dtype=f32).reshape(64, 1),
        "cosc": cosT, "sinc": sinT, "rotc": rotT, "tric": tri,
        "k0t": np.ascontiguousarray(k0[0:D]),
        "k0b": np.ascontiguousarray(k0[D:2 * D]),
        "k1": np.ascontiguousarray(np.asarray(inp["L1_ffn_k1"]).view(np.uint8)),
        "k2": np.ascontiguousarray(np.asarray(inp["L1_ffn_k2"]).view(np.uint8)),
        "s0": np.asarray(inp["L1_ffn_s0"], dtype=f32).reshape(1, 1),
        "s1": np.asarray(inp["L1_ffn_s1"], dtype=f32).reshape(1, 1),
        "s2": np.asarray(inp["L1_ffn_s2"], dtype=f32).reshape(1, 1),
    }
    return com


def _consts():
    fcc = np.tile(np.array([0.125, 2.0, -1.0, EPS, 1.0 / D, 1.0 / DH, 1.0, 4.0],
                           dtype=np.float32), (128, 1))
    ones_bf = np.ones((128, 128), dtype=BF16NP)
    ones_f = np.ones((1, 128), dtype=np.float32)
    pos = np.arange(L, dtype=np.float64)
    di = np.arange(0, DH, 2, dtype=np.float64) / DH
    theta = pos[None, :] * np.exp(-di[:, None] * np.log(10000.0))
    cosT = np.repeat(np.cos(theta), 2, axis=0).astype(BF16NP)
    sinT = np.repeat(np.sin(theta), 2, axis=0).astype(BF16NP)
    R = np.zeros((DH, DH), dtype=np.float32)
    for d in range(32):
        R[d, 2 * d + 1] = -1.0
    for d in range(32, 64):
        R[d, 2 * (d - 32)] = 1.0
    rotT = np.ascontiguousarray(R.T).astype(BF16NP)
    tri = np.where(np.arange(128)[:, None] <= np.arange(128)[None, :], 0.0, NEG).astype(np.float32)
    return ones_bf, ones_f, cosT, sinT, rotT, tri, fcc


def _get_nc(name, builder):
    if name not in _NC_CACHE:
        _NC_CACHE[name] = builder()
    return _NC_CACHE[name]


def kernel(**inp):
    n = 8
    f32 = np.float32
    com = _phase_arrays(inp)
    base = {"onesc": com["onesc"], "onesf": com["onesf"], "fcc": com["fcc"]}
    for (pre, _, specs) in PHASES:
        for (nm, _, _) in specs:
            base[pre + nm] = com[nm]
    nc = _get_nc("fused", build_fused)
    in_maps = []
    for b in range(n):
        m = dict(base)
        m["xT"] = np.ascontiguousarray(inp["L0_attn_x"][b].T).astype(f32, copy=False)
        in_maps.append(m)
    res = run_bass_kernel_spmd(nc, in_maps, core_ids=list(range(n)))
    out = np.stack([np.ascontiguousarray(r["xout"].T) for r in res.results], axis=0)
    return out.astype(np.float32, copy=False)
